# version 06
# baseline (speedup 1.0000x reference)
import numpy as np
from contextlib import ExitStack

import concourse.bass as bass
import concourse.mybir as mybir
from concourse.bass_utils import run_bass_kernel_spmd

F32 = mybir.dt.float32
BF16 = mybir.dt.bfloat16
AF = mybir.ActivationFunctionType
ALU = mybir.AluOpType
AX = mybir.AxisListType.X

T = 2048
D = 2048
NTC = 16
NKC = 16
NE = 64
I_DIM = 512
DKV = 512
LN8 = float(np.log(0.125))

ENGINES = ("pe", "act", "dve", "sp", "pool")


def ts(i, n=128):
    return slice(i * n, (i + 1) * n)


class Builder:
    def __init__(self, nc):
        self.nc = nc
        self.q = {e: [] for e in ENGINES}
        self.es = ExitStack()
        self.sem = {}
        self.cnt = {}
        self.seen = {}

    def new_sem(self, name):
        h = self.es.enter_context(self.nc.semaphore(name))
        self.sem[name] = h
        self.cnt[name] = 0
        return name

    def sbuf(self, name, shape, dtype=F32):
        return self.es.enter_context(self.nc.sbuf_tensor(name, shape, dtype))

    DMA_SEMS = ("s_cst", "s_gwt", "s_wg0", "s_wg1", "s_wu0", "s_wu1",
                "s_wdn", "s_xa", "s_st", "s_dbg")

    def _filter(self, eng, waits):
        w = []
        for ent in waits:
            if ent is None:
                continue
            s, c = ent
            if s in self.DMA_SEMS:
                c = self.cnt[s]
            if c <= 0:
                continue
            k = (eng, s)
            if self.seen.get(k, 0) >= c:
                continue
            self.seen[k] = c
            w.append((s, c))
        return w

    def op(self, eng, fn, waits=(), inc=None, n=1):
        if inc == "s_dbg":
            waits = list(waits) + [("s_dbg", self.cnt["s_dbg"])]
        w = self._filter(eng, waits)
        out = None
        if inc is not None:
            self.cnt[inc] += n
            out = (inc, self.cnt[inc])
        sems = self.sem
        def run(e, _w=tuple(w), _fn=fn, _inc=inc, _n=n):
            for s, c in _w:
                e.wait_ge(sems[s], c)
            r = _fn(e)
            if _inc is not None:
                r.then_inc(sems[_inc], _n)
        self.q[eng].append(run)
        return out

    def wait(self, eng, deps):
        w = self._filter(eng, deps)
        if not w:
            return
        sems = self.sem
        def run(e, _w=tuple(w)):
            for s, c in _w:
                e.wait_ge(sems[s], c)
        self.q[eng].append(run)

    def phase_barrier(self):
        deps = [(s, self.cnt[s]) for s in self.cnt]
        for eng in ENGINES:
            self.wait(eng, deps)

    def finalize(self):
        nc = self.nc
        with nc.Block() as block:
            @block.tensor
            def _(t):
                for f in self.q["pe"]:
                    f(t)
            @block.scalar
            def _(s):
                for f in self.q["act"]:
                    f(s)
            @block.vector
            def _(v):
                for f in self.q["dve"]:
                    f(v)
            @block.sync
            def _(s):
                for f in self.q["sp"]:
                    f(s)
            @block.gpsimd
            def _(g):
                for f in self.q["pool"]:
                    f(g)


def build_program(debug=False, n_experts=NE, phases=99, gh_limit=32, gh_list=None, gh_partial=99):
    nc = bass.Bass("TRN2", target_bir_lowering=False)
    B = Builder(nc)

    def din(name, shape, dtype=F32):
        return nc.dram_tensor(name, shape, dtype, kind="ExternalInput").ap()

    x_in = din("x", [T, D])
    wq_r = din("wq_r", [16, D, 128])
    wk_r = din("wk_r", [4, D, 128])
    wv_r = din("wv_r", [4, D, 128])
    m_aug_in = din("m_aug", [DKV, DKV + 1])
    mw_in = din("mw", [1, 1])
    gate_w = din("gate_w", [D, NE])
    eg_r = din("eg_r", [NE, 4, D, 128])
    eu_r = din("eu_r", [NE, 4, D, 128])
    ed_in = din("ed", [NE, I_DIM, D])
    sg_r = din("sg_r", [4, D, 128])
    su_r = din("su_r", [4, D, 128])
    sd_in = din("sd", [I_DIM, D])
    seg_in = din("seg", [128, 16])
    consts_in = din("consts_i", [128, 8])
    idn_in = din("idn_i", [128, 128])
    ones_row_in = din("ones_row_i", [1, 128])

    out_d = nc.dram_tensor("out", [T, D], F32, kind="ExternalOutput").ap()
    x_res = nc.dram_tensor("x_res", [T, D], F32, kind="Internal").ap()
    x_res2 = nc.dram_tensor("x_res2", [T, D], F32, kind="Internal").ap()
    kv_scr = nc.dram_tensor("kv_scr", [8 * 128, T], BF16, kind="Internal").ap()

    dbg = {}
    def dout(name, shape, dtype=F32):
        if debug:
            dbg[name] = nc.dram_tensor("dbg_" + name, shape, dtype,
                                       kind="ExternalOutput").ap()

    dout("xlnfm", [128, NKC * T], BF16)
    dout("c", [128, NTC * NE])
    dout("lg", [128, 1024])
    dout("sg", [128, 16])
    dout("acc", [128, NTC * D], BF16)
    dout("att", [128, NTC * D], BF16)
    dout("xln2", [128, NKC * T], BF16)
    dout("q", [128, NKC * T], BF16)
    dout("adsb", [65, T])
    dout("negm", [128, 32])
    dout("sig", [128, 8192], BF16)
    dout("amem", [128, NTC * D], BF16)
    dout("smf", [128, 16])
    dout("num0", [128, 1024])
    dout("adsb0", [65, T])
    dout("attmid", [128, NTC * D], BF16)
    dout("Edump", [128, 2048], BF16)

    for s in ("s_pe", "s_act", "s_dve", "s_cst", "s_gwt", "s_wg0", "s_wg1",
              "s_wu0", "s_wu1", "s_wdn", "s_xa", "s_st", "s_dbg"):
        B.new_sem(s)

    R1b = B.sbuf("R1b", [128, 32768], BF16)
    R2b = B.sbuf("R2b", [128, 32768], BF16)
    R3b = B.sbuf("R3b", [128, 32768], BF16)
    R4f = B.sbuf("R4f", [128, 3072], F32)

    consts = B.sbuf("consts", [128, 8], F32)
    idn_bf = B.sbuf("idn_bf", [128, 128], BF16)
    idn_f = B.sbuf("idn_f", [65, 65], F32)
    seg_sb = B.sbuf("seg_sb", [128, NKC], BF16)
    smf = B.sbuf("smf", [128, 16], F32)
    srow = B.sbuf("srow", [1, 208], F32)
    ones_row = srow[0:1, 80:208]
    wmw = B.sbuf("wmw", [1, 1], F32)
    statb = [B.sbuf(f"statb{i}", [128, 12], F32) for i in range(2)]
    t8b = B.sbuf("t8b", [128, 8], F32)
    rstw = B.sbuf("rstw", [128, 8], F32)
    ebuf = B.sbuf("ebuf", [128, NE], F32)
    esel = B.sbuf("esel", [128, NE], BF16)
    sg_tm = B.sbuf("sg_tm", [128, NTC], F32)
    negm_b = B.sbuf("negm_b", [128, 32], F32)
    rzs16 = B.sbuf("rzs16", [128, 32], F32)

    nc.const_aps.register(F32, 1e-5, consts[:, 4:5])
    nc.const_aps.register(F32, 0.125, consts[:, 5:6])
    nc.const_aps.register(F32, 0.5, consts[:, 2:3])
    nc.const_aps.register(F32, LN8, consts[:, 6:7])

    xlnfm = R1b
    accb = R2b
    wg_v = [R3b[:, 0:2048], R3b[:, 2048:4096]]
    wu_v = [R3b[:, 4096:6144], R3b[:, 6144:8192]]
    wd_v = R3b[:, 8192:16384]
    h_v = R3b[:, 16384:24576]
    tmps_v = [R3b[:, 24576:25600], R3b[:, 25600:26624]]
    tmpd_v = [R3b[:, 26624:27648], R3b[:, 27648:28672]]
    ln_v = [R3b[:, 28672:30720], R3b[:, 30720:32768]]
    dump_bf = R3b[:, 24576:26624]
    q_sb = R3b
    k_v = R1b[:, 0:8192]
    vT_off = 8192
    sig_v = R1b[:, 16512:24704]
    E_v = R1b[:, 24704:26752]
    E_v2 = R1b[:, 28804:30852]
    msb_v = R1b[:, 26752:28804]
    adsb = R4f[0:65, 0:2048]
    kvst_v = [R2b[:, 0:2048], R2b[:, 2048:4096]]
    wqkv_v = [R2b[:, 4096:6144], R2b[:, 6144:8192]]
    vtmp_v = R2b[:, 0:2048]
    xst_v = [R4f[:, 0:2048], R4f[:, 0:2048]]
    lsb_v = R4f[0:64, 0:2048]
    lg_v = R4f[:, 0:1024]
    c_v = R4f[:, 2048:3072]
    t3_v = [R4f[:, 0:1024], R4f[:, 1024:2048], R4f[:, 2048:3072]]
    qsq_v = R4f[0:64, 0:2048]
    gw_v = R3b[:, 0:1024]

    c_consts = B.op("sp", lambda e: e.dma_start(out=consts[:], in_=consts_in),
                    inc="s_cst", n=16)
    c_idnf = B.op("sp", lambda e: e.dma_start(out=idn_f[:], in_=idn_in[0:65, 0:65]),
                  inc="s_cst", n=16)
    c_ones = B.op("sp", lambda e: e.dma_start(out=ones_row, in_=ones_row_in),
                  inc="s_cst", n=16)
    c_mw = B.op("sp", lambda e: e.dma_start(out=wmw[:], in_=mw_in),
                inc="s_cst", n=16)
    c_idnb = B.op("pool", lambda e: e.dma_start(out=idn_bf[:], in_=idn_in),
                  inc="s_gwt", n=16)
    c_seg = B.op("pool", lambda e: e.dma_start(
        out=seg_sb[:], in_=seg_in), inc="s_gwt", n=16)

    B.wait("act", [c_consts])
    B.wait("dve", [c_consts])

    e_wsig = B.op("act", lambda e: e.activation(out=srow[0:1, 64:65], in_=wmw[:],
                                                func=AF.Sigmoid),
                  waits=[c_mw], inc="s_act")
    with nc.psum_tensor("ps_wb", [128, 1], F32) as ps_wb:
        e_wbp = B.op("pe", lambda e: e.matmul(ps_wb[:], ones_row,
                                              srow[0:1, 64:65],
                                              start=True, stop=True),
                     waits=[e_wsig, c_ones], inc="s_pe")
        e_wb = B.op("dve", lambda e: e.tensor_copy(smf[:, 0:1], ps_wb[:]),
                    waits=[e_wbp], inc="s_dve")
    e_omwb = B.op("dve", lambda e: e.tensor_scalar(
        out=smf[:, 1:2], in0=smf[:, 0:1], scalar1=-1.0, scalar2=1.0,
        op0=ALU.mult, op1=ALU.add), waits=[e_wb], inc="s_dve")
    wb_c = smf[:, 0:1]
    omwb_c = smf[:, 1:2]

    bstate = {"stage_free": [], "tps_free": [None, None], "dump_last": None}

    def emit_boundary(x_old_dram, add_src, add_scale, add_deps, store_dram,
                      tps=None, final=False, pre_ld=None):
        last = None
        evs = []
        for tc in range(NTC):
            sl = tc % 2
            st_ = statb[sl]
            xn = xst_v[sl]
            ld = B.op("sp", lambda e, tc=tc, sl=sl, a=x_old_dram: e.dma_start(
                out=xst_v[sl], in_=a[ts(tc), :]),
                waits=bstate["stage_free"]
                + ([pre_ld] if tc == 0 and pre_ld is not None else []),
                inc="s_xa", n=16)
            frees = []
            if add_src is not None:
                e_xn = B.op("dve", lambda e, tc=tc, sl=sl: e.scalar_tensor_tensor(
                    out=xst_v[sl], in0=add_src[:, tc * D:(tc + 1) * D],
                    scalar=add_scale, in1=xst_v[sl],
                    op0=ALU.mult, op1=ALU.add, accum_out=st_[:, 0:1]),
                    waits=[ld] + list(add_deps), inc="s_dve")
                src_ready = e_xn
                if store_dram is not None:
                    e_st = B.op("sp", lambda e, tc=tc, sl=sl, d=store_dram:
                                e.dma_start(out=d[ts(tc), :], in_=xst_v[sl]),
                                waits=[e_xn], inc="s_st", n=16)
                    frees.append(e_st)
                    last = e_st
            else:
                src_ready = ld
                e_id = B.op("act", lambda e, sl=sl: e.activation(
                    out=dump_bf[:], in_=xst_v[sl], func=AF.Identity,
                    accum_out=st_[:, 0:1]),
                    waits=[ld, bstate["dump_last"]], inc="s_act")
                bstate["dump_last"] = e_id
            if final:
                bstate["stage_free"] = frees
                continue
            e_sq = B.op("act", lambda e, sl=sl: e.activation(
                out=dump_bf[:], in_=xst_v[sl], func=AF.Square,
                accum_out=st_[:, 1:2]),
                waits=[src_ready, bstate["dump_last"]], inc="s_act")
            bstate["dump_last"] = e_sq
            e_mean = B.op("dve", lambda e, sl=sl: e.tensor_scalar_mul(
                st_[:, 2:3], st_[:, 0:1], 1.0 / D),
                waits=[src_ready if add_src is not None else None, e_sq],
                inc="s_dve")
            e_x2 = B.op("dve", lambda e, sl=sl: e.tensor_scalar_mul(
                st_[:, 3:4], st_[:, 1:2], 1.0 / D), inc="s_dve")
            e_msq = B.op("dve", lambda e, sl=sl: e.tensor_mul(
                st_[:, 4:5], st_[:, 2:3], st_[:, 2:3]),
                waits=[e_mean], inc="s_dve")
            e_var = B.op("dve", lambda e, sl=sl: e.tensor_sub(
                st_[:, 5:6], st_[:, 3:4], st_[:, 4:5]),
                waits=[e_x2, e_msq], inc="s_dve")
            e_sd = B.op("act", lambda e, sl=sl: e.activation(
                out=st_[:, 6:7], in_=st_[:, 5:6], func=AF.Sqrt, bias=1e-5),
                waits=[e_var], inc="s_act")
            e_rstd = B.op("dve", lambda e, sl=sl: e.reciprocal(
                st_[:, 7:8], st_[:, 6:7]), waits=[e_sd], inc="s_dve")
            e_nmr = B.op("dve", lambda e, sl=sl: e.scalar_tensor_tensor(
                out=st_[:, 8:9], in0=st_[:, 2:3], scalar=-1.0, in1=st_[:, 7:8],
                op0=ALU.mult, op1=ALU.mult), waits=[e_rstd], inc="s_dve")
            e_ln = B.op("act", lambda e, sl=sl: e.activation(
                out=ln_v[sl], in_=xst_v[sl], func=AF.Identity,
                bias=st_[:, 8:9], scale=st_[:, 7:8]),
                waits=[e_nmr], inc="s_act")
            frees.append(e_ln)
            bstate["stage_free"] = frees
            def tr(e, sl=sl):
                r = None
                for j in range(NKC):
                    r = e.transpose(tps[sl][:, ts(j)], ln_v[sl][:, ts(j)],
                                    idn_bf[:])
                return r
            e_tr = B.op("pe", tr, waits=[e_ln, c_idnb, bstate["tps_free"][sl]],
                        inc="s_pe")
            e_ev = B.op("dve", lambda e, sl=sl, tc=tc: e.tensor_copy(
                xlnfm[:].rearrange("p (k t) -> p k t", k=NKC)[:, :, ts(tc)],
                tps[sl][:].rearrange("p (k t) -> p k t", k=NKC)),
                waits=[e_tr], inc="s_dve")
            bstate["tps_free"][sl] = e_ev
            evs.append(e_ev)
            last = e_ev
        if not final:
            bstate["xln_half"] = [evs[7], evs[15]]
        return last

    uid = [0]
    def _u(base):
        uid[0] += 1
        return f"{base}_{uid[0]}"

    def emit_router(xln_ready, xln_half=None, pre=None):
        c_gw = B.op("pool", lambda e: e.dma_start(
            out=gw_v.rearrange("p (k e) -> p k e", k=NKC),
            in_=gate_w.rearrange("(k p) e -> p k e", p=128)),
            waits=[pre], inc="s_gwt", n=16)
        xh = xln_half if xln_half is not None else [xln_ready, xln_ready]
        with ExitStack() as ph:
            lp = ph.enter_context(nc.psum_tensor(_u("lp"), [64, T], F32))
            ltp = ph.enter_context(nc.psum_tensor(_u("ltp"), [128, 1024], F32))
            sgp = ph.enter_context(nc.psum_tensor(_u("sgp"), [128, NTC], F32))
            xv = xlnfm[:].rearrange("p (k t) -> p k t", k=NKC)
            gv = gw_v.rearrange("p (k e) -> p k e", k=NKC)
            def lmm(e, th=0):
                r = None
                for nt in (2 * th, 2 * th + 1):
                    for kk in range(NKC):
                        r = e.matmul(lp[:, nt * 512:(nt + 1) * 512],
                                     gv[:, kk, :],
                                     xv[:, kk, nt * 512:(nt + 1) * 512],
                                     start=(kk == 0), stop=(kk == NKC - 1))
                return r
            e_lp0 = B.op("pe", lambda e: lmm(e, 0), waits=[xh[0], c_gw],
                         inc="s_pe")
            e_lp = B.op("pe", lambda e: lmm(e, 1), waits=[xh[1]],
                        inc="s_pe")
            e_lsb = B.op("dve", lambda e: e.tensor_copy(lsb_v, lp[:]),
                         waits=[e_lp0, e_lp], inc="s_dve")
            def ltr(e):
                r = None
                for tc in range(NTC):
                    r = e.transpose(ltp[:, tc * 64:(tc + 1) * 64],
                                    lsb_v[:, ts(tc)], idn_f[0:64, 0:64])
                return r
            e_ltr = B.op("pe", ltr, waits=[e_lsb, c_idnf], inc="s_pe")
            e_lg = B.op("dve", lambda e: e.tensor_copy(lg_v, ltp[:]),
                        waits=[e_ltr], inc="s_dve")
            def sgm(e, th=0):
                r = None
                for tc in range(8 * th, 8 * th + 8):
                    for kk in range(NKC):
                        r = e.matmul(sgp[:, tc:tc + 1], xv[:, kk, ts(tc)],
                                     seg_sb[:, kk:kk + 1],
                                     start=(kk == 0), stop=(kk == NKC - 1))
                return r
            e_sgm0 = B.op("pe", lambda e: sgm(e, 0), waits=[xh[0], c_seg],
                          inc="s_pe")
            e_sgm = B.op("pe", lambda e: sgm(e, 1), waits=[xh[1]],
                         inc="s_pe")
            e_sg = B.op("act", lambda e: e.activation(
                out=sg_tm[:], in_=sgp[:], func=AF.Sigmoid),
                waits=[e_sgm0, e_sgm], inc="s_act")
            c_ready = None
            prev_E = None
            for tc in range(NTC):
                lg = lg_v[:, tc * 64:(tc + 1) * 64]
                e_t8 = B.op("dve", lambda e, lg=lg: e.max(out=t8b[:], in_=lg),
                            waits=[e_lg], inc="s_dve")
                e_rm = B.op("dve", lambda e, lg=lg: e.reduce_max(
                    rstw[:, 0:1], lg, axis=AX), inc="s_dve")
                B.op("dve", lambda e, lg=lg: e.tensor_scalar(
                    out=esel[:], in0=lg, scalar1=t8b[:, 7:8], scalar2=None,
                    op0=ALU.is_ge), waits=[e_t8])
                e_nm = B.op("dve", lambda e: e.tensor_scalar_mul(
                    rstw[:, 1:2], rstw[:, 0:1], -1.0),
                    waits=[e_rm, prev_E], inc="s_dve")
                e_E = B.op("act", lambda e, lg=lg: e.activation(
                    out=ebuf[:], in_=lg, func=AF.Exp, bias=rstw[:, 1:2]),
                    waits=[e_nm], inc="s_act")
                prev_E = e_E
                e_z = B.op("dve", lambda e: e.scalar_tensor_tensor(
                    out=ebuf[:], in0=ebuf[:], scalar=1.0, in1=esel[:],
                    op0=ALU.mult, op1=ALU.mult, accum_out=rstw[:, 2:3]),
                    waits=[e_E], inc="s_dve")
                e_rz = B.op("dve", lambda e: e.reciprocal(
                    rstw[:, 3:4], rstw[:, 2:3]), waits=[e_z], inc="s_dve")
                c_ready = B.op("dve", lambda e, tc=tc: e.tensor_scalar_mul(
                    c_v[:, tc * 64:(tc + 1) * 64], ebuf[:], rstw[:, 3:4]),
                    waits=[e_rz], inc="s_dve")
            return c_ready, e_sg, e_lp

    wstate = {"g_free": [None, None], "u_free": [None, None], "d_free": None,
              "tmps_free": [None, None], "tmpd_free": [None, None],
              "h_free": None}

    def emit_ffn(xln_ready, c_ready, sg_ready, router_pe, xln_half=None):
        xh = xln_half if xln_half is not None else [xln_ready, xln_ready]
        with ExitStack() as ph:
            pA = ph.enter_context(nc.psum_tensor(_u("pA"), [128, 1024], F32))
            pB = ph.enter_context(nc.psum_tensor(_u("pB"), [128, 1024], F32))
            pC = [ph.enter_context(nc.psum_tensor(_u(f"pC{i}"), [128, 1024], F32))
                  for i in range(2)]
            psA_free = None
            psB_free = None
            psC_free = [None, None]
            xv = xlnfm[:].rearrange("p (k t) -> p k t", k=NKC)
            acc_last = None

            def one_expert(g_dram, u_dram, d_dram, scale_fn, is_write):
                nonlocal psA_free, psB_free, acc_last
                h_last = None
                gu_done = []
                for m in range(4):
                    sl = m % 2
                    e_g = B.op("pool", lambda e, m=m, sl=sl, g=g_dram:
                               e.dma_start(
                                   out=wg_v[sl].rearrange("p (k i) -> p k i",
                                                          k=NKC),
                                   in_=g[m].rearrange("(k p) i -> p k i",
                                                      p=128)),
                               waits=[wstate["g_free"][sl], router_pe],
                               inc="s_wg0" if sl == 0 else "s_wg1", n=16)
                    e_u = B.op("pool", lambda e, m=m, sl=sl, u=u_dram:
                               e.dma_start(
                                   out=wu_v[sl].rearrange("p (k i) -> p k i",
                                                          k=NKC),
                                   in_=u[m].rearrange("(k p) i -> p k i",
                                                      p=128)),
                               waits=[wstate["u_free"][sl]],
                               inc="s_wu0" if sl == 0 else "s_wu1", n=16)
                    e_gmm = e_umm = None
                    for th in range(2):
                        toff = th * 1024
                        def gmm(e, sl=sl, toff=toff):
                            r = None
                            for nt in range(2):
                                for kk in range(NKC):
                                    r = e.matmul(
                                        pA[:, nt * 512:(nt + 1) * 512],
                                        wg_v[sl][:, ts(kk)],
                                        xv[:, kk, toff + nt * 512:
                                           toff + (nt + 1) * 512],
                                        start=(kk == 0), stop=(kk == NKC - 1))
                            return r
                        e_gmm = B.op("pe", gmm,
                                     waits=[e_g, xh[th], psA_free],
                                     inc="s_pe")
                        def umm(e, sl=sl, toff=toff):
                            r = None
                            for nt in range(2):
                                for kk in range(NKC):
                                    r = e.matmul(
                                        pB[:, nt * 512:(nt + 1) * 512],
                                        wu_v[sl][:, ts(kk)],
                                        xv[:, kk, toff + nt * 512:
                                           toff + (nt + 1) * 512],
                                        start=(kk == 0), stop=(kk == NKC - 1))
                            return r
                        e_umm = B.op("pe", umm, waits=[e_u, psB_free],
                                     inc="s_pe")
                        ssl = th
                        e_sil = B.op("act", lambda e, ssl=ssl: e.activation(
                            out=tmps_v[ssl], in_=pA[:], func=AF.Silu),
                            waits=[e_gmm, wstate["tmps_free"][ssl]],
                            inc="s_act")
                        psA_free = e_sil
                        e_h = B.op("dve", lambda e, m=m, toff=toff, ssl=ssl:
                                   e.tensor_tensor(
                                       out=h_v[:, m * T + toff:
                                               m * T + toff + 1024],
                                       in0=tmps_v[ssl], in1=pB[:],
                                       op=ALU.mult),
                                   waits=[e_sil, e_umm, wstate["h_free"]],
                                   inc="s_dve")
                        psB_free = e_h
                        wstate["tmps_free"][ssl] = e_h
                        h_last = e_h
                    wstate["g_free"][sl] = e_gmm
                    wstate["u_free"][sl] = e_umm
                e_dw = B.op("pool", lambda e, dd=d_dram: e.dma_start(
                    out=wd_v.rearrange("p (k h) -> p k h", k=4),
                    in_=dd.rearrange("(k p) h -> p k h", p=128)),
                    waits=[wstate["d_free"]], inc="s_wdn", n=16)
                d_pe_last = None
                for tc in range(NTC):
                    for hf in range(2):
                        s2 = (tc * 2 + hf) % 2
                        hoff = hf * 1024
                        def dmm(e, tc=tc, s2=s2, hoff=hoff):
                            r = None
                            for nt in range(2):
                                for kk in range(4):
                                    r = e.matmul(
                                        pC[s2][:, nt * 512:(nt + 1) * 512],
                                        h_v[:, kk * T + tc * 128:
                                            kk * T + (tc + 1) * 128],
                                        wd_v[:, kk * D + hoff + nt * 512:
                                             kk * D + hoff + (nt + 1) * 512],
                                        start=(kk == 0), stop=(kk == 3))
                            return r
                        e_dmm = B.op("pe", dmm,
                                     waits=[e_dw, h_last, psC_free[s2]],
                                     inc="s_pe")
                        d_pe_last = e_dmm
                        e_sc = B.op("act", lambda e, s2=s2, sc=scale_fn(tc):
                                    e.activation(out=tmpd_v[s2], in_=pC[s2][:],
                                                 func=AF.Copy, scale=sc),
                                    waits=[e_dmm, wstate["tmpd_free"][s2],
                                           c_ready, sg_ready],
                                    inc="s_act")
                        psC_free[s2] = e_sc
                        dst = accb[:, tc * D + hoff:tc * D + hoff + 1024]
                        if is_write:
                            e_acc = B.op("dve", lambda e, dst=dst, s2=s2:
                                         e.tensor_copy(dst, tmpd_v[s2]),
                                         waits=[e_sc], inc="s_dve")
                        else:
                            e_acc = B.op("dve", lambda e, dst=dst, s2=s2:
                                         e.tensor_add(dst, dst, tmpd_v[s2]),
                                         waits=[e_sc], inc="s_dve")
                        wstate["tmpd_free"][s2] = e_acc
                        acc_last = e_acc
                wstate["h_free"] = d_pe_last
                wstate["d_free"] = d_pe_last

            one_expert([sg_r[m] for m in range(4)], [su_r[m] for m in range(4)],
                       sd_in, lambda tc: sg_tm[:, tc:tc + 1], True)
            for ei in range(n_experts):
                one_expert([eg_r[ei, m] for m in range(4)],
                           [eu_r[ei, m] for m in range(4)], ed_in[ei],
                           lambda tc, ei=ei: c_v[:, tc * 64 + ei:
                                                 tc * 64 + ei + 1], False)
            return acc_last

    def emit_attention(xln_ready):
        with ExitStack() as ph:
            QP = [ph.enter_context(nc.psum_tensor(_u(f"QP{i}"), [128, 2048], F32))
                  for i in range(2)]
            xv = xlnfm[:].rearrange("p (k t) -> p k t", k=NKC)
            qp_free = [None, None]
            w_free = [None, None]
            kv_st_free = [None, None]
            q_ready = None
            kv_stored = []
            plan = ([("k", i, wk_r) for i in range(4)]
                    + [("v", i, wv_r) for i in range(4)]
                    + [("q", i, wq_r) for i in range(16)])
            for j, (kind, idx, wsrc) in enumerate(plan):
                sl = j % 2
                e_w = B.op("pool", lambda e, sl=sl, idx=idx, w=wsrc:
                           e.dma_start(
                               out=wqkv_v[sl].rearrange("p (k i) -> p k i",
                                                        k=NKC),
                               in_=w[idx].rearrange("(k p) i -> p k i", p=128)),
                           waits=[w_free[sl]],
                           inc="s_wg0" if sl == 0 else "s_wg1", n=16)
                def pmm(e, sl=sl):
                    r = None
                    for nt in range(4):
                        for kk in range(NKC):
                            r = e.matmul(
                                QP[sl][:, nt * 512:(nt + 1) * 512],
                                wqkv_v[sl][:, ts(kk)],
                                xv[:, kk, nt * 512:(nt + 1) * 512],
                                start=(kk == 0), stop=(kk == NKC - 1))
                    return r
                e_mm = B.op("pe", pmm, waits=[e_w, xln_ready, qp_free[sl]],
                            inc="s_pe")
                w_free[sl] = e_mm
                if kind == "q":
                    e_ev = B.op("dve", lambda e, sl=sl, idx=idx: e.tensor_copy(
                        q_sb[:, idx * T:(idx + 1) * T], QP[sl][:]),
                        waits=[e_mm], inc="s_dve")
                else:
                    row0 = (0 if kind == "k" else 512) + idx * 128
                    e_ev = B.op("dve", lambda e, sl=sl: e.tensor_copy(
                        kvst_v[sl], QP[sl][:]),
                        waits=[e_mm, kv_st_free[sl]], inc="s_dve")
                    e_st = B.op("sp", lambda e, sl=sl, row0=row0: e.dma_start(
                        out=kv_scr[row0:row0 + 128, :], in_=kvst_v[sl]),
                        waits=[e_ev], inc="s_st", n=16)
                    kv_st_free[sl] = e_st
                    kv_stored.append(e_st)
                qp_free[sl] = e_ev
                if kind == "q":
                    q_ready = e_ev
        B.phase_barrier()

        e_k = B.op("sp", lambda e: e.dma_start(
            out=k_v.rearrange("p (c t) -> p c t", c=4),
            in_=kv_scr[0:512, :].rearrange("(c p) t -> p c t", p=128)),
            waits=kv_stored[-1:], inc="s_xa", n=16)
        e_msb = B.op("pool", lambda e: e.dma_start(
            out=msb_v.rearrange("p (k j) -> p k j", k=4),
            in_=m_aug_in.rearrange("(k p) j -> p k j", p=128)),
            inc="s_gwt", n=16)
        vT_ready = None
        with ExitStack() as ph:
            TPv = ph.enter_context(nc.psum_tensor(_u("TPv"), [128, 2048], BF16))
            vtmp_free = None
            tpv_free = None
            for vc in range(4):
                e_vl = B.op("sp", lambda e, vc=vc: e.dma_start(
                    out=vtmp_v, in_=kv_scr[512 + vc * 128:512 + (vc + 1) * 128, :]),
                    waits=[kv_stored[-1], vtmp_free], inc="s_xa", n=16)
                def vtr(e):
                    r = None
                    for tc in range(NTC):
                        r = e.transpose(TPv[:, ts(tc)], vtmp_v[:, ts(tc)],
                                        idn_bf[:])
                    return r
                e_vt = B.op("pe", vtr, waits=[e_vl, tpv_free], inc="s_pe")
                vtmp_free = e_vt
                def vev(e, vc=vc):
                    r = None
                    for tc in range(NTC):
                        r = e.tensor_copy(
                            R1b[:, vT_off + tc * 520 + vc * 130:
                                vT_off + tc * 520 + vc * 130 + 130]
                            .rearrange("p (h d) -> p h d", h=2),
                            TPv[:, tc * 128:tc * 128 + 128]
                            .rearrange("p (h d) -> p h d", h=2)[:, :, 0:64]
                            if False else
                            TPv[:, tc * 128:tc * 128 + 128]
                            .rearrange("p (h d) -> p h d", h=2))
                    return r
                def vev2(e, vc=vc):
                    r = None
                    for tc in range(NTC):
                        dst = R1b[:, vT_off + tc * 520 + vc * 130:
                                  vT_off + tc * 520 + (vc + 1) * 130]
                        r = e.tensor_copy(
                            dst.rearrange("p (h d) -> p h d", d=65)[:, :, 0:64],
                            TPv[:, tc * 128:(tc + 1) * 128]
                            .rearrange("p (h d) -> p h d", h=2))
                    return r
                e_vv = B.op("dve", vev2, waits=[e_vt], inc="s_dve")
                tpv_free = e_vv
                vT_ready = e_vv
            def vones(e):
                r = None
                for tc in range(NTC):
                    dst = R1b[:, vT_off + tc * 520:vT_off + (tc + 1) * 520]
                    r = e.memset(
                        dst.rearrange("p (h d) -> p h d", d=65)[:, :, 64:65],
                        1.0)
                return r
            vT_ready = B.op("dve", vones, waits=[vT_ready], inc="s_dve")

            MP = [ph.enter_context(nc.psum_tensor(_u(f"MP{i}"), [128, 1024], F32))
                  for i in range(2)]
            mp_free = [None, None]
            att_mem = []
            prev_rdw = None
            dbg_hold = [None]
            for g in range(4):
                sig_ready = None
                sig_waits = [att_mem[-1] if att_mem else None, dbg_hold[0]]
                prev_e2 = None
                for i in range(4):
                    qc = 4 * g + i
                    for hv in range(2):
                        off = hv * 1024
                        qs = q_sb[:, qc * T + off:qc * T + off + 1024]
                        B.op("dve", lambda e, qs=qs: e.tensor_scalar_min(
                            t3_v[0], qs, 0.0),
                            waits=[q_ready, prev_e2, sig_ready] + sig_waits)
                        e_ex = B.op("dve", lambda e, qs=qs: e.tensor_scalar_max(
                            t3_v[1], qs, 0.0), waits=[sig_ready], inc="s_dve")
                        e_e2 = B.op("act", lambda e: e.activation(
                            out=t3_v[2], in_=t3_v[0], func=AF.Exp),
                            waits=[e_ex], inc="s_act")
                        prev_e2 = e_e2
                        sig_ready = B.op("dve", lambda e, i=i, off=off:
                                         e.scalar_tensor_tensor(
                                             out=sig_v[:, i * T + off:
                                                       i * T + off + 1024],
                                             in0=t3_v[2], scalar=-1.0,
                                             in1=t3_v[1],
                                             op0=ALU.add, op1=ALU.add),
                                         waits=[e_e2], inc="s_dve")
                mv = msb_v.rearrange("p (k j) -> p k j", k=4)
                for tc in range(NTC):
                    s2 = tc % 2
                    def mmm(e, g=g, tc=tc, s2=s2):
                        r = None
                        for kk in range(4):
                            r = e.matmul(MP[s2][:, 0:512],
                                         sig_v[:, kk * T + tc * 128:
                                               kk * T + (tc + 1) * 128],
                                         mv[:, kk, 0:512],
                                         start=(kk == 0), stop=(kk == 3))
                        for kk in range(4):
                            r = e.matmul(MP[s2][:, 512:513],
                                         sig_v[:, kk * T + tc * 128:
                                               kk * T + (tc + 1) * 128],
                                         mv[:, kk, 512:513],
                                         start=(kk == 0), stop=(kk == 3))
                        return r
                    e_mm = B.op("pe", mmm,
                                waits=[sig_ready, e_msb, mp_free[s2]],
                                inc="s_pe")
                    if debug and g == 0 and tc == 0:
                        e_n0 = B.op("dve", lambda e, s2=s2: e.tensor_copy(
                            t3_v[0].rearrange("p f -> p f") if False else
                            R4f[:, 1024:2048][:, 0:513], MP[s2][:, 0:513]),
                            waits=[e_mm], inc="s_dve")
                        e_n0d = B.op("sp", lambda e: e.dma_start(
                            out=dbg["num0"], in_=R4f[:, 1024:2048]),
                            waits=[e_n0], inc="s_dbg", n=16)
                        dbg_hold[0] = e_n0d
                    e_rd = B.op("dve", lambda e, s2=s2: e.reciprocal(
                        rstw[:, 4:5], MP[s2][:, 512:513]),
                        waits=[e_mm, prev_rdw], inc="s_dve")
                    e_rdw = B.op("dve", lambda e: e.tensor_mul(
                        rstw[:, 5:6], rstw[:, 4:5], wb_c),
                        waits=[e_rd, e_wb,
                               att_mem[-1] if att_mem else None],
                        inc="s_dve")
                    prev_rdw = e_rdw
                    e_am = B.op("act", lambda e, g=g, tc=tc, s2=s2:
                                e.activation(
                                    out=accb[:, tc * D + g * 512:
                                             tc * D + (g + 1) * 512],
                                    in_=MP[s2][:, 0:512], func=AF.Copy,
                                    scale=rstw[:, 5:6]),
                                waits=[e_rdw, vT_ready], inc="s_act")
                    mp_free[s2] = e_am
                    att_mem.append(e_am)
            att_mem_last = att_mem[-1]
            if debug:
                att_mem_last = B.op("sp", lambda e: e.dma_start(
                    out=dbg["amem"], in_=accb[:]),
                    waits=[att_mem_last], inc="s_dbg", n=16)


        with ExitStack() as ph:
            QN2 = [ph.enter_context(nc.psum_tensor(_u(f"QN{i}"), [1, T], F32))
                   for i in range(2)]
            qsq2 = [E_v[0:64, :], E_v2[0:64, :]]
            qb_ones = B.sbuf(_u("qb_ones"), [64, 1], BF16)
            e_qb1 = B.op("dve", lambda e: e.memset(qb_ones[:], 1.0),
                         inc="s_dve")
            items = []
            for h in range(8):
                kb2 = (h % 2) * 64
                items.append((k_v.rearrange("p (c t) -> p c t", c=4)
                              [kb2:kb2 + 64, h // 2, :], e_k, 32 + h))
            for gh in range(32):
                qc2, qb2 = gh // 2, (gh % 2) * 64
                items.append((q_sb[qb2:qb2 + 64, qc2 * T:(qc2 + 1) * T],
                              q_ready, 0 + gh))
            e_nm_l = [None] * 40
            e_rx_l = [None] * 40
            for i, (src, dep, dsti) in enumerate(items):
                bb = i % 2
                e_q2 = B.op("act", lambda e, src=src, bb=bb: e.activation(
                    out=qsq2[bb], in_=src, func=AF.Square),
                    waits=[dep, e_nm_l[i - 2] if i >= 2 else None],
                    inc="s_act")
                def nmm(e, bb=bb):
                    r = None
                    for nt in range(4):
                        r = e.matmul(QN2[bb][:, nt * 512:(nt + 1) * 512],
                                     qb_ones[:],
                                     qsq2[bb][:, nt * 512:(nt + 1) * 512],
                                     start=True, stop=True)
                    return r
                e_nm_l[i] = B.op("pe", nmm,
                                 waits=[e_q2, e_qb1,
                                        e_rx_l[i - 2] if i >= 2 else None],
                                 inc="s_pe")
                e_rx_l[i] = B.op("dve", lambda e, bb=bb, dsti=dsti:
                                 e.reduce_max(srow[0:1, dsti:dsti + 1],
                                              QN2[bb][:], axis=AX),
                                 waits=[e_nm_l[i]], inc="s_dve")
            qn_free = e_rx_l[-1]
        with ExitStack() as ph:
            PBC = ph.enter_context(nc.psum_tensor(_u("PBC"), [128, 32], F32))
            e_rep = B.op("dve", lambda e: e.tensor_copy(
                srow[0:1, 40:72].rearrange("p (g h) -> p g h", g=4),
                srow[0:1, 32:40].unsqueeze(1).to_broadcast([1, 4, 8])), waits=[qn_free], inc="s_dve")
            e_prod = B.op("dve", lambda e: e.tensor_mul(
                srow[0:1, 40:72], srow[0:1, 40:72], srow[0:1, 0:32]),
                waits=[e_rep], inc="s_dve")
            e_ln2 = B.op("act", lambda e: e.activation(
                out=srow[0:1, 0:32], in_=srow[0:1, 40:72], func=AF.Ln),
                waits=[e_prod], inc="s_act")
            e_bnd = B.op("act", lambda e: e.activation(
                out=srow[0:1, 40:72], in_=srow[0:1, 0:32], func=AF.Exp,
                scale=0.5, bias=LN8), waits=[e_ln2], inc="s_act")
            e_neg = B.op("dve", lambda e: e.tensor_scalar_mul(
                srow[0:1, 0:32], srow[0:1, 40:72], -1.0),
                waits=[e_bnd], inc="s_dve")
            e_bc = B.op("pe", lambda e: e.matmul(
                PBC[:], ones_row, srow[0:1, 0:32], start=True, stop=True),
                waits=[e_neg], inc="s_pe")
            e_nb = B.op("dve", lambda e: e.tensor_copy(negm_b[:], PBC[:]),
                        waits=[e_bc], inc="s_dve")

        att_last = att_mem_last
        prev_stt = None
        prev_ad = None
        ghs = list(gh_list if gh_list is not None else range(gh_limit))
        for ghi, gh in enumerate(ghs):
            g, h = gh // 8, gh % 8
            qc, qb = gh // 2, (gh % 2) * 64
            kc, kb = h // 2, (h % 2) * 64
            with ExitStack() as ph:
                ST = [ph.enter_context(nc.psum_tensor(_u(f"ST{i}"),
                                                      [128, 1024], F32))
                      for i in range(2)]
                ADp = ph.enter_context(nc.psum_tensor(_u("AD"), [65, T], F32))
                st_free = [None, None]
                ad_last = prev_ad
                Ev = [E_v, E_v2]
                e_exp = [[None, None] for _ in range(NTC)]
                e_av = [[None, None] for _ in range(NTC)]

                def emit_avm(tc, half):
                    def avm(e, tc=tc, half=half, h=h, ADp=ADp):
                        rr = None
                        for nt in (2 * half, 2 * half + 1):
                            rr = e.matmul(
                                ADp[:, nt * 512:(nt + 1) * 512],
                                R1b[:, vT_off + tc * 520 + h * 65:
                                    vT_off + tc * 520 + h * 65 + 65],
                                Ev[tc % 2][:, nt * 512:(nt + 1) * 512],
                                start=(tc == 0), stop=(tc == NTC - 1))
                        return rr
                    e_av[tc][half] = B.op(
                        "pe", avm,
                        waits=[e_exp[tc][half], vT_ready,
                               prev_stt if tc == 0 else None],
                        inc="s_pe")

                for tc in range(NTC):
                    for half in range(2):
                        sl = half
                        def smm(e, tc=tc, half=half, sl=sl, ST=ST,
                                kb=kb, kc=kc, qb=qb, qc=qc):
                            rr = None
                            for nt in range(2):
                                so = half * 1024 + nt * 512
                                rr = e.matmul(
                                    ST[sl][:, nt * 512:(nt + 1) * 512],
                                    k_v[kb:kb + 64, kc * T + tc * 128:
                                        kc * T + (tc + 1) * 128],
                                    q_sb[qb:qb + 64,
                                         qc * T + so:qc * T + so + 512],
                                    start=True, stop=True)
                            return rr
                        e_st = B.op("pe", smm,
                                    waits=[q_ready, e_k, st_free[sl], e_nb],
                                    inc="s_pe")
                        e_exp[tc][half] = B.op(
                            "act", lambda e, half=half, sl=sl, gh=gh,
                            tc=tc, ST=ST: e.activation(
                                out=Ev[tc % 2][:, half * 1024:
                                               (half + 1) * 1024],
                                in_=ST[sl][:], func=AF.Exp,
                                scale=0.125,
                                bias=negm_b[:, gh:gh + 1]),
                            waits=[e_st, ad_last,
                                   e_av[tc - 2][half] if tc >= 2 else None],
                            inc="s_act")
                        st_free[sl] = e_exp[tc][half]
                    if tc >= 1:
                        emit_avm(tc - 1, 0)
                        emit_avm(tc - 1, 1)
                emit_avm(NTC - 1, 0)
                emit_avm(NTC - 1, 1)
                ad_last = e_av[NTC - 1][1]
                if debug and ghi == 0:
                    B.op("sp", lambda e: e.dma_start(
                        out=dbg["Edump"], in_=E_v), waits=[ad_last],
                        inc="s_dbg", n=16)
                e_ad = B.op("dve", lambda e, ADp=ADp: e.tensor_copy(
                    adsb, ADp[:]), waits=[ad_last], inc="s_dve")
                if debug and ghi == 0:
                    e_ad = B.op("sp", lambda e: e.dma_start(
                        out=dbg["adsb0"], in_=adsb), waits=[e_ad],
                        inc="s_dbg", n=16)
                prev_ad = e_ad
            with ExitStack() as ph2:
                dummy = ph2.enter_context(nc.psum_tensor(_u("dm"),
                                                         [128, 2048], F32))
                TP = ph2.enter_context(nc.psum_tensor(_u("TP"),
                                                      [128, 2048], F32))
                def adtr(e, TP=TP):
                    r = None
                    for tc in range(NTC):
                        r = e.transpose(TP[:, tc * 128:tc * 128 + 65],
                                        adsb[:, ts(tc)], idn_f[0:65, 0:65])
                    return r
                e_tr = B.op("pe", adtr, waits=[e_ad, c_idnf], inc="s_pe")
                e_rz1 = B.op("dve", lambda e, TP=TP: e.reciprocal(
                    rzs16[:, 0:16],
                    TP[:].rearrange("p (c x) -> p c x", x=128)[:, :, 64:65]
                    .squeeze(2)), waits=[e_tr], inc="s_dve")
                e_rzs = B.op("dve", lambda e: e.tensor_scalar(
                    out=rzs16[:, 16:32], in0=rzs16[:, 0:16],
                    scalar1=omwb_c, scalar2=None, op0=ALU.mult),
                    waits=[e_rz1, e_omwb], inc="s_dve")
                for tc in range(NTC):
                    dsl = accb[:, tc * D + g * 512 + h * 64:
                               tc * D + g * 512 + h * 64 + 64]
                    prev_stt = B.op("dve", lambda e, tc=tc, dsl=dsl, TP=TP:
                                    e.scalar_tensor_tensor(
                                        out=dsl,
                                        in0=TP[:, tc * 128:tc * 128 + 64],
                                        scalar=rzs16[:, 16 + tc:17 + tc],
                                        in1=dsl,
                                        op0=ALU.mult, op1=ALU.add),
                                    waits=[e_rzs, att_mem_last], inc="s_dve")
                att_last = prev_stt
                if debug and ghi == 0:
                    prev_stt = B.op("sp", lambda e: e.dma_start(
                        out=dbg["attmid"], in_=accb[:]), waits=[prev_stt],
                        inc="s_dbg", n=16)
        return att_last

    with ExitStack() as phb:
        tps = [phb.enter_context(nc.psum_tensor(f"tps{i}", [128, 2048], BF16))
               for i in range(2)]
        xln_ready = emit_boundary(x_in, None, None, [], None, tps=tps)
    if debug:
        B.op("sp", lambda e: e.dma_start(out=dbg["xlnfm"], in_=xlnfm[:]),
             waits=[xln_ready], inc="s_dbg", n=16)

    acc_ready = None
    if phases >= 2:
        c_ready, sg_ready, router_pe_last = emit_router(
            xln_ready, xln_half=bstate["xln_half"])
        acc_ready = emit_ffn(xln_ready, c_ready, sg_ready, router_pe_last,
                             xln_half=bstate["xln_half"])
        if debug:
            B.op("sp", lambda e: e.dma_start(out=dbg["c"], in_=c_v),
                 waits=[c_ready], inc="s_dbg", n=16)
            B.op("sp", lambda e: e.dma_start(out=dbg["lg"], in_=lg_v),
                 waits=[c_ready], inc="s_dbg", n=16)
            B.op("sp", lambda e: e.dma_start(out=dbg["sg"], in_=sg_tm[:]),
                 waits=[sg_ready], inc="s_dbg", n=16)
            B.op("sp", lambda e: e.dma_start(out=dbg["acc"], in_=accb[:]),
                 waits=[acc_ready], inc="s_dbg", n=16)

    if phases >= 3:
        B.phase_barrier()
        with ExitStack() as phb:
            tps = [phb.enter_context(
                nc.psum_tensor(f"tp3{i}", [128, 2048], BF16)) for i in range(2)]
            xln_ready = emit_boundary(x_in, accb[:], 0.5, [acc_ready], x_res,
                                      tps=tps)
        if debug:
            B.op("sp", lambda e: e.dma_start(out=dbg["xln2"], in_=xlnfm[:]),
                 waits=[xln_ready], inc="s_dbg", n=16)

    att_ready = None
    if phases >= 4:
        B.phase_barrier()
        att_ready = emit_attention(xln_ready)
        if debug:
            B.op("sp", lambda e: e.dma_start(out=dbg["q"], in_=q_sb[:]),
                 waits=[att_ready], inc="s_dbg", n=16)
            B.op("sp", lambda e: e.dma_start(out=dbg["adsb"], in_=adsb),
                 waits=[att_ready], inc="s_dbg", n=16)
            B.op("sp", lambda e: e.dma_start(out=dbg["negm"], in_=negm_b[:]),
                 waits=[att_ready], inc="s_dbg", n=16)
            B.op("sp", lambda e: e.dma_start(out=dbg["sig"], in_=sig_v),
                 waits=[att_ready], inc="s_dbg", n=16)
            B.op("sp", lambda e: e.dma_start(out=dbg["att"], in_=accb[:]),
                 waits=[att_ready], inc="s_dbg", n=16)

    if phases >= 5:
        B.phase_barrier()
        with ExitStack() as phb:
            tps = [phb.enter_context(
                nc.psum_tensor(f"tp5{i}", [128, 2048], BF16)) for i in range(2)]
            xln_ready = emit_boundary(x_res, accb[:], 1.0, [att_ready],
                                      x_res2, tps=tps)

    if phases >= 6:
        c_ready, sg_ready, router_pe_last = emit_router(
            xln_ready, xln_half=bstate["xln_half"], pre=att_ready)
        acc_ready = emit_ffn(xln_ready, c_ready, sg_ready, router_pe_last,
                             xln_half=bstate["xln_half"])

    if phases >= 7:
        out_last = emit_boundary(x_res2, accb[:], 0.5, [acc_ready], out_d,
                                 final=True, pre_ld=c_ready)
        B.wait("sp", [out_last])

    B.wait("sp", [(s, B.cnt[s]) for s in
                  ("s_st", "s_xa", "s_cst", "s_dbg")])
    B.wait("pool", [(s, B.cnt[s]) for s in
                    ("s_gwt", "s_wg0", "s_wg1", "s_wu0", "s_wu1", "s_wdn")])

    B.finalize()
    B.es.close()
    return nc, dbg



def _prep_inputs(ins):
    f32 = np.float32
    d = {}
    d["wq_r"] = np.ascontiguousarray(
        ins["L0_attn_Wq"].T.reshape(D, 16, 128).transpose(1, 0, 2))
    d["wk_r"] = np.ascontiguousarray(
        ins["L0_attn_Wk"].T.reshape(D, 4, 128).transpose(1, 0, 2))
    d["wv_r"] = np.ascontiguousarray(
        ins["L0_attn_Wv"].T.reshape(D, 4, 128).transpose(1, 0, 2))
    d["m_aug"] = np.ascontiguousarray(np.concatenate(
        [ins["L0_attn_memory"], ins["L0_attn_memory_norm"]], axis=1))
    d["mw"] = np.asarray(ins["L0_attn_memory_weight"]).reshape(1, 1).astype(f32)
    d["gate_w"] = ins["L0_ffn_gate_w"]
    d["eg_r"] = np.ascontiguousarray(
        ins["L0_ffn_expert_gate_w"].reshape(NE, D, 4, 128).transpose(0, 2, 1, 3))
    d["eu_r"] = np.ascontiguousarray(
        ins["L0_ffn_expert_up_w"].reshape(NE, D, 4, 128).transpose(0, 2, 1, 3))
    d["ed"] = ins["L0_ffn_expert_down_w"]
    d["sg_r"] = np.ascontiguousarray(
        ins["L0_ffn_shared_gate_w"].reshape(D, 4, 128).transpose(1, 0, 2))
    d["su_r"] = np.ascontiguousarray(
        ins["L0_ffn_shared_up_w"].reshape(D, 4, 128).transpose(1, 0, 2))
    d["sd"] = ins["L0_ffn_shared_down_w"]
    d["seg"] = np.ascontiguousarray(
        ins["L0_ffn_shared_expert_gate_w"].reshape(16, 128).T)
    consts = np.zeros((128, 8), f32)
    consts[:, 0] = 1.0
    consts[:, 1] = -1.0
    consts[:, 2] = 0.5
    consts[:, 3] = 1.0 / D
    consts[:, 4] = 1e-5
    consts[:, 5] = 0.125
    consts[:, 6] = LN8
    d["consts_i"] = consts
    d["idn_i"] = np.eye(128, dtype=f32)
    d["ones_row_i"] = np.ones((1, 128), f32)
    return d


def kernel(**inputs):
    nc, _ = build_program()
    shared = _prep_inputs(inputs)
    x = inputs["L0_attn_x"]
    in_maps = []
    for c in range(8):
        m = dict(shared)
        m["x"] = np.ascontiguousarray(x[c])
        in_maps.append(m)
    res = run_bass_kernel_spmd(nc, in_maps, core_ids=list(range(8)))
    return np.stack([res.results[c]["out"] for c in range(8)], axis=0)


# version 07 (final)
# speedup vs baseline: 1.0223x; 1.0223x over previous
import numpy as np
from contextlib import ExitStack

import concourse.bass as bass
import concourse.mybir as mybir
from concourse.bass_utils import run_bass_kernel_spmd

F32 = mybir.dt.float32
BF16 = mybir.dt.bfloat16
AF = mybir.ActivationFunctionType
ALU = mybir.AluOpType
AX = mybir.AxisListType.X

T = 2048
D = 2048
NTC = 16
NKC = 16
NE = 64
I_DIM = 512
DKV = 512
LN8 = float(np.log(0.125))

ENGINES = ("pe", "act", "dve", "sp", "pool")


def ts(i, n=128):
    return slice(i * n, (i + 1) * n)


class Builder:
    def __init__(self, nc):
        self.nc = nc
        self.q = {e: [] for e in ENGINES}
        self.es = ExitStack()
        self.sem = {}
        self.cnt = {}
        self.seen = {}

    def new_sem(self, name):
        h = self.es.enter_context(self.nc.semaphore(name))
        self.sem[name] = h
        self.cnt[name] = 0
        return name

    def sbuf(self, name, shape, dtype=F32):
        return self.es.enter_context(self.nc.sbuf_tensor(name, shape, dtype))

    DMA_SEMS = ("s_cst", "s_gwt", "s_wg0", "s_wg1", "s_wu0", "s_wu1",
                "s_wdn", "s_xa", "s_st", "s_dbg")

    def _filter(self, eng, waits):
        w = []
        for ent in waits:
            if ent is None:
                continue
            s, c = ent
            if s in self.DMA_SEMS:
                c = self.cnt[s]
            if c <= 0:
                continue
            k = (eng, s)
            if self.seen.get(k, 0) >= c:
                continue
            self.seen[k] = c
            w.append((s, c))
        return w

    def op(self, eng, fn, waits=(), inc=None, n=1):
        if inc == "s_dbg":
            waits = list(waits) + [("s_dbg", self.cnt["s_dbg"])]
        w = self._filter(eng, waits)
        out = None
        if inc is not None:
            self.cnt[inc] += n
            out = (inc, self.cnt[inc])
        sems = self.sem
        def run(e, _w=tuple(w), _fn=fn, _inc=inc, _n=n):
            for s, c in _w:
                e.wait_ge(sems[s], c)
            r = _fn(e)
            if _inc is not None:
                r.then_inc(sems[_inc], _n)
        self.q[eng].append(run)
        return out

    def wait(self, eng, deps):
        w = self._filter(eng, deps)
        if not w:
            return
        sems = self.sem
        def run(e, _w=tuple(w)):
            for s, c in _w:
                e.wait_ge(sems[s], c)
        self.q[eng].append(run)

    def phase_barrier(self):
        deps = [(s, self.cnt[s]) for s in self.cnt]
        for eng in ENGINES:
            self.wait(eng, deps)

    def finalize(self):
        nc = self.nc
        with nc.Block() as block:
            @block.tensor
            def _(t):
                for f in self.q["pe"]:
                    f(t)
            @block.scalar
            def _(s):
                for f in self.q["act"]:
                    f(s)
            @block.vector
            def _(v):
                for f in self.q["dve"]:
                    f(v)
            @block.sync
            def _(s):
                for f in self.q["sp"]:
                    f(s)
            @block.gpsimd
            def _(g):
                for f in self.q["pool"]:
                    f(g)


def build_program(debug=False, n_experts=NE, phases=99, gh_limit=32, gh_list=None, gh_partial=99):
    nc = bass.Bass("TRN2", target_bir_lowering=False)
    B = Builder(nc)

    def din(name, shape, dtype=F32):
        return nc.dram_tensor(name, shape, dtype, kind="ExternalInput").ap()

    x_in = din("x", [T, D])
    wq_r = din("wq_r", [16, D, 128])
    wk_r = din("wk_r", [4, D, 128])
    wv_r = din("wv_r", [4, D, 128])
    m_aug_in = din("m_aug", [DKV, DKV + 1])
    mw_in = din("mw", [1, 1])
    gate_w = din("gate_w", [D, NE])
    eg_r = din("eg_r", [NE, 4, D, 128])
    eu_r = din("eu_r", [NE, 4, D, 128])
    ed_in = din("ed", [NE, I_DIM, D])
    sg_r = din("sg_r", [4, D, 128])
    su_r = din("su_r", [4, D, 128])
    sd_in = din("sd", [I_DIM, D])
    seg_in = din("seg", [128, 16])
    consts_in = din("consts_i", [128, 8])
    idn_in = din("idn_i", [128, 128])
    ones_row_in = din("ones_row_i", [1, 128])

    out_d = nc.dram_tensor("out", [T, D], F32, kind="ExternalOutput").ap()
    x_res = nc.dram_tensor("x_res", [T, D], F32, kind="Internal").ap()
    x_res2 = nc.dram_tensor("x_res2", [T, D], F32, kind="Internal").ap()
    kv_scr = nc.dram_tensor("kv_scr", [8 * 128, T], BF16, kind="Internal").ap()

    dbg = {}
    def dout(name, shape, dtype=F32):
        if debug:
            dbg[name] = nc.dram_tensor("dbg_" + name, shape, dtype,
                                       kind="ExternalOutput").ap()

    dout("xlnfm", [128, NKC * T], BF16)
    dout("c", [128, NTC * NE])
    dout("lg", [128, 1024])
    dout("sg", [128, 16])
    dout("acc", [128, NTC * D], BF16)
    dout("att", [128, NTC * D], BF16)
    dout("xln2", [128, NKC * T], BF16)
    dout("q", [128, NKC * T], BF16)
    dout("adsb", [65, T])
    dout("negm", [128, 32])
    dout("sig", [128, 8192], BF16)
    dout("amem", [128, NTC * D], BF16)
    dout("smf", [128, 16])
    dout("num0", [128, 1024])
    dout("adsb0", [65, T])
    dout("attmid", [128, NTC * D], BF16)
    dout("Edump", [128, 2048], BF16)

    for s in ("s_pe", "s_act", "s_dve", "s_cst", "s_gwt", "s_wg0", "s_wg1",
              "s_wu0", "s_wu1", "s_wdn", "s_xa", "s_st", "s_dbg"):
        B.new_sem(s)

    R1b = B.sbuf("R1b", [128, 32768], BF16)
    R2b = B.sbuf("R2b", [128, 32768], BF16)
    R3b = B.sbuf("R3b", [128, 32768], BF16)
    R4f = B.sbuf("R4f", [128, 3072], F32)

    consts = B.sbuf("consts", [128, 8], F32)
    idn_bf = B.sbuf("idn_bf", [128, 128], BF16)
    idn_f = B.sbuf("idn_f", [65, 65], F32)
    seg_sb = B.sbuf("seg_sb", [128, NKC], BF16)
    smf = B.sbuf("smf", [128, 16], F32)
    srow = B.sbuf("srow", [1, 208], F32)
    ones_row = srow[0:1, 80:208]
    wmw = B.sbuf("wmw", [1, 1], F32)
    statb = [B.sbuf(f"statb{i}", [128, 12], F32) for i in range(2)]
    t8b = B.sbuf("t8b", [128, 8], F32)
    rstw = B.sbuf("rstw", [128, 8], F32)
    ebuf = B.sbuf("ebuf", [128, NE], F32)
    esel = B.sbuf("esel", [128, NE], BF16)
    sg_tm = B.sbuf("sg_tm", [128, NTC], F32)
    negm_b = B.sbuf("negm_b", [128, 32], F32)
    rzs16 = B.sbuf("rzs16", [128, 32], F32)

    nc.const_aps.register(F32, 1e-5, consts[:, 4:5])
    nc.const_aps.register(F32, 0.125, consts[:, 5:6])
    nc.const_aps.register(F32, 0.5, consts[:, 2:3])
    nc.const_aps.register(F32, LN8, consts[:, 6:7])

    xlnfm = R1b
    accb = R2b
    wg_v = [R3b[:, 0:2048], R3b[:, 2048:4096]]
    wu_v = [R3b[:, 4096:6144], R3b[:, 6144:8192]]
    wd_v = R3b[:, 8192:16384]
    h_v = R3b[:, 16384:24576]
    tmps_v = [R3b[:, 24576:25600], R3b[:, 25600:26624]]
    tmpd_v = [R3b[:, 26624:27648], R3b[:, 27648:28672]]
    ln_v = [R3b[:, 28672:30720], R3b[:, 30720:32768]]
    dump_bf = R3b[:, 24576:26624]
    q_sb = R3b
    k_v = R1b[:, 0:8192]
    vT_off = 8192
    sig_v = R1b[:, 16512:24704]
    E_v = R1b[:, 24704:26752]
    E_v2 = R1b[:, 28804:30852]
    msb_v = R1b[:, 26752:28804]
    adsb = R4f[0:65, 0:2048]
    kvst_v = [R2b[:, 0:2048], R2b[:, 2048:4096]]
    wqkv_v = [R2b[:, 4096:6144], R2b[:, 6144:8192]]
    vtmp_v = R2b[:, 0:2048]
    xst_v = [R4f[:, 0:2048], R4f[:, 0:2048]]
    lsb_v = R4f[0:64, 0:2048]
    lg_v = R4f[:, 0:1024]
    c_v = R4f[:, 2048:3072]
    t3_v = [R4f[:, 0:1024], R4f[:, 1024:2048], R4f[:, 2048:3072]]
    qsq_v = R4f[0:64, 0:2048]
    gw_v = R3b[:, 0:1024]

    c_consts = B.op("sp", lambda e: e.dma_start(out=consts[:], in_=consts_in),
                    inc="s_cst", n=16)
    c_idnf = B.op("sp", lambda e: e.dma_start(out=idn_f[:], in_=idn_in[0:65, 0:65]),
                  inc="s_cst", n=16)
    c_ones = B.op("sp", lambda e: e.dma_start(out=ones_row, in_=ones_row_in),
                  inc="s_cst", n=16)
    c_mw = B.op("sp", lambda e: e.dma_start(out=wmw[:], in_=mw_in),
                inc="s_cst", n=16)
    c_idnb = B.op("pool", lambda e: e.dma_start(out=idn_bf[:], in_=idn_in),
                  inc="s_gwt", n=16)
    c_seg = B.op("pool", lambda e: e.dma_start(
        out=seg_sb[:], in_=seg_in), inc="s_gwt", n=16)

    B.wait("act", [c_consts])
    B.wait("dve", [c_consts])

    e_wsig = B.op("act", lambda e: e.activation(out=srow[0:1, 64:65], in_=wmw[:],
                                                func=AF.Sigmoid),
                  waits=[c_mw], inc="s_act")
    with nc.psum_tensor("ps_wb", [128, 1], F32) as ps_wb:
        e_wbp = B.op("pe", lambda e: e.matmul(ps_wb[:], ones_row,
                                              srow[0:1, 64:65],
                                              start=True, stop=True),
                     waits=[e_wsig, c_ones], inc="s_pe")
        e_wb = B.op("dve", lambda e: e.tensor_copy(smf[:, 0:1], ps_wb[:]),
                    waits=[e_wbp], inc="s_dve")
    e_omwb = B.op("dve", lambda e: e.tensor_scalar(
        out=smf[:, 1:2], in0=smf[:, 0:1], scalar1=-1.0, scalar2=1.0,
        op0=ALU.mult, op1=ALU.add), waits=[e_wb], inc="s_dve")
    wb_c = smf[:, 0:1]
    omwb_c = smf[:, 1:2]

    bstate = {"stage_free": [], "tps_free": [None, None], "dump_last": None}

    def emit_boundary(x_old_dram, add_src, add_scale, add_deps, store_dram,
                      tps=None, final=False, pre_ld=None):
        last = None
        evs = []
        for tc in range(NTC):
            sl = tc % 2
            st_ = statb[sl]
            xn = xst_v[sl]
            ld = B.op("sp", lambda e, tc=tc, sl=sl, a=x_old_dram: e.dma_start(
                out=xst_v[sl], in_=a[ts(tc), :]),
                waits=bstate["stage_free"]
                + ([pre_ld] if tc == 0 and pre_ld is not None else []),
                inc="s_xa", n=16)
            frees = []
            if add_src is not None:
                e_xn = B.op("dve", lambda e, tc=tc, sl=sl: e.scalar_tensor_tensor(
                    out=xst_v[sl], in0=add_src[:, tc * D:(tc + 1) * D],
                    scalar=add_scale, in1=xst_v[sl],
                    op0=ALU.mult, op1=ALU.add, accum_out=st_[:, 0:1]),
                    waits=[ld] + list(add_deps), inc="s_dve")
                src_ready = e_xn
                if store_dram is not None:
                    e_st = B.op("sp", lambda e, tc=tc, sl=sl, d=store_dram:
                                e.dma_start(out=d[ts(tc), :], in_=xst_v[sl]),
                                waits=[e_xn], inc="s_st", n=16)
                    frees.append(e_st)
                    last = e_st
            else:
                src_ready = ld
                e_id = B.op("act", lambda e, sl=sl: e.activation(
                    out=dump_bf[:], in_=xst_v[sl], func=AF.Identity,
                    accum_out=st_[:, 0:1]),
                    waits=[ld, bstate["dump_last"]], inc="s_act")
                bstate["dump_last"] = e_id
            if final:
                bstate["stage_free"] = frees
                continue
            e_sq = B.op("act", lambda e, sl=sl: e.activation(
                out=dump_bf[:], in_=xst_v[sl], func=AF.Square,
                accum_out=st_[:, 1:2]),
                waits=[src_ready, bstate["dump_last"]], inc="s_act")
            bstate["dump_last"] = e_sq
            e_mean = B.op("dve", lambda e, sl=sl: e.tensor_scalar_mul(
                st_[:, 2:3], st_[:, 0:1], 1.0 / D),
                waits=[src_ready if add_src is not None else None, e_sq],
                inc="s_dve")
            e_x2 = B.op("dve", lambda e, sl=sl: e.tensor_scalar_mul(
                st_[:, 3:4], st_[:, 1:2], 1.0 / D), inc="s_dve")
            e_msq = B.op("dve", lambda e, sl=sl: e.tensor_mul(
                st_[:, 4:5], st_[:, 2:3], st_[:, 2:3]),
                waits=[e_mean], inc="s_dve")
            e_var = B.op("dve", lambda e, sl=sl: e.tensor_sub(
                st_[:, 5:6], st_[:, 3:4], st_[:, 4:5]),
                waits=[e_x2, e_msq], inc="s_dve")
            e_sd = B.op("act", lambda e, sl=sl: e.activation(
                out=st_[:, 6:7], in_=st_[:, 5:6], func=AF.Sqrt, bias=1e-5),
                waits=[e_var], inc="s_act")
            e_rstd = B.op("dve", lambda e, sl=sl: e.reciprocal(
                st_[:, 7:8], st_[:, 6:7]), waits=[e_sd], inc="s_dve")
            e_nmr = B.op("dve", lambda e, sl=sl: e.scalar_tensor_tensor(
                out=st_[:, 8:9], in0=st_[:, 2:3], scalar=-1.0, in1=st_[:, 7:8],
                op0=ALU.mult, op1=ALU.mult), waits=[e_rstd], inc="s_dve")
            e_ln = B.op("act", lambda e, sl=sl: e.activation(
                out=ln_v[sl], in_=xst_v[sl], func=AF.Identity,
                bias=st_[:, 8:9], scale=st_[:, 7:8]),
                waits=[e_nmr], inc="s_act")
            frees.append(e_ln)
            bstate["stage_free"] = frees
            def tr(e, sl=sl):
                r = None
                for j in range(NKC):
                    r = e.transpose(tps[sl][:, ts(j)], ln_v[sl][:, ts(j)],
                                    idn_bf[:])
                return r
            e_tr = B.op("pe", tr, waits=[e_ln, c_idnb, bstate["tps_free"][sl]],
                        inc="s_pe")
            e_ev = B.op("dve", lambda e, sl=sl, tc=tc: e.tensor_copy(
                xlnfm[:].rearrange("p (k t) -> p k t", k=NKC)[:, :, ts(tc)],
                tps[sl][:].rearrange("p (k t) -> p k t", k=NKC)),
                waits=[e_tr], inc="s_dve")
            bstate["tps_free"][sl] = e_ev
            evs.append(e_ev)
            last = e_ev
        if not final:
            bstate["xln_half"] = [evs[7], evs[15]]
        return last

    uid = [0]
    def _u(base):
        uid[0] += 1
        return f"{base}_{uid[0]}"

    def emit_router(xln_ready, xln_half=None, pre=None):
        gwt_prev = ("s_gwt", B.cnt["s_gwt"])
        c_gw = B.op("pool", lambda e: e.dma_start(
            out=gw_v.rearrange("p (k e) -> p k e", k=NKC),
            in_=gate_w.rearrange("(k p) e -> p k e", p=128)),
            waits=[pre, gwt_prev], inc="s_gwt", n=16)
        xh = xln_half if xln_half is not None else [xln_ready, xln_ready]
        with ExitStack() as ph:
            lp = ph.enter_context(nc.psum_tensor(_u("lp"), [64, T], F32))
            ltp = ph.enter_context(nc.psum_tensor(_u("ltp"), [128, 1024], F32))
            sgp = ph.enter_context(nc.psum_tensor(_u("sgp"), [128, NTC], F32))
            xv = xlnfm[:].rearrange("p (k t) -> p k t", k=NKC)
            gv = gw_v.rearrange("p (k e) -> p k e", k=NKC)
            def lmm(e, th=0):
                r = None
                for nt in (2 * th, 2 * th + 1):
                    for kk in range(NKC):
                        r = e.matmul(lp[:, nt * 512:(nt + 1) * 512],
                                     gv[:, kk, :],
                                     xv[:, kk, nt * 512:(nt + 1) * 512],
                                     start=(kk == 0), stop=(kk == NKC - 1))
                return r
            e_lp0 = B.op("pe", lambda e: lmm(e, 0), waits=[xh[0], c_gw],
                         inc="s_pe")
            e_lp = B.op("pe", lambda e: lmm(e, 1), waits=[xh[1]],
                        inc="s_pe")
            e_lsb = B.op("dve", lambda e: e.tensor_copy(lsb_v, lp[:]),
                         waits=[e_lp0, e_lp], inc="s_dve")
            def ltr(e):
                r = None
                for tc in range(NTC):
                    r = e.transpose(ltp[:, tc * 64:(tc + 1) * 64],
                                    lsb_v[:, ts(tc)], idn_f[0:64, 0:64])
                return r
            e_ltr = B.op("pe", ltr, waits=[e_lsb, c_idnf], inc="s_pe")
            e_lg = B.op("dve", lambda e: e.tensor_copy(lg_v, ltp[:]),
                        waits=[e_ltr], inc="s_dve")
            def sgm(e, th=0):
                r = None
                for tc in range(8 * th, 8 * th + 8):
                    for kk in range(NKC):
                        r = e.matmul(sgp[:, tc:tc + 1], xv[:, kk, ts(tc)],
                                     seg_sb[:, kk:kk + 1],
                                     start=(kk == 0), stop=(kk == NKC - 1))
                return r
            e_sgm0 = B.op("pe", lambda e: sgm(e, 0), waits=[xh[0], c_seg],
                          inc="s_pe")
            e_sgm = B.op("pe", lambda e: sgm(e, 1), waits=[xh[1]],
                         inc="s_pe")
            e_sg = B.op("act", lambda e: e.activation(
                out=sg_tm[:], in_=sgp[:], func=AF.Sigmoid),
                waits=[e_sgm0, e_sgm], inc="s_act")
            c_ready = None
            prev_E = None
            for tc in range(NTC):
                lg = lg_v[:, tc * 64:(tc + 1) * 64]
                e_t8 = B.op("dve", lambda e, lg=lg: e.max(out=t8b[:], in_=lg),
                            waits=[e_lg], inc="s_dve")
                e_rm = B.op("dve", lambda e, lg=lg: e.reduce_max(
                    rstw[:, 0:1], lg, axis=AX), inc="s_dve")
                B.op("dve", lambda e, lg=lg: e.tensor_scalar(
                    out=esel[:], in0=lg, scalar1=t8b[:, 7:8], scalar2=None,
                    op0=ALU.is_ge), waits=[e_t8])
                e_nm = B.op("dve", lambda e: e.tensor_scalar_mul(
                    rstw[:, 1:2], rstw[:, 0:1], -1.0),
                    waits=[e_rm, prev_E], inc="s_dve")
                e_E = B.op("act", lambda e, lg=lg: e.activation(
                    out=ebuf[:], in_=lg, func=AF.Exp, bias=rstw[:, 1:2]),
                    waits=[e_nm], inc="s_act")
                prev_E = e_E
                e_z = B.op("dve", lambda e: e.scalar_tensor_tensor(
                    out=ebuf[:], in0=ebuf[:], scalar=1.0, in1=esel[:],
                    op0=ALU.mult, op1=ALU.mult, accum_out=rstw[:, 2:3]),
                    waits=[e_E], inc="s_dve")
                e_rz = B.op("dve", lambda e: e.reciprocal(
                    rstw[:, 3:4], rstw[:, 2:3]), waits=[e_z], inc="s_dve")
                c_ready = B.op("dve", lambda e, tc=tc: e.tensor_scalar_mul(
                    c_v[:, tc * 64:(tc + 1) * 64], ebuf[:], rstw[:, 3:4]),
                    waits=[e_rz], inc="s_dve")
            return c_ready, e_sg, e_lp

    wstate = {"g_free": [None, None], "u_free": [None, None], "d_free": None,
              "tmps_free": [None, None], "tmpd_free": [None, None],
              "h_free": None}

    def emit_ffn(xln_ready, c_ready, sg_ready, router_pe, xln_half=None):
        xh = xln_half if xln_half is not None else [xln_ready, xln_ready]
        with ExitStack() as ph:
            pA = ph.enter_context(nc.psum_tensor(_u("pA"), [128, 1024], F32))
            pB = ph.enter_context(nc.psum_tensor(_u("pB"), [128, 1024], F32))
            pC = [ph.enter_context(nc.psum_tensor(_u(f"pC{i}"), [128, 1024], F32))
                  for i in range(2)]
            psA_free = None
            psB_free = None
            psC_free = [None, None]
            xv = xlnfm[:].rearrange("p (k t) -> p k t", k=NKC)
            acc_last = None

            def one_expert(g_dram, u_dram, d_dram, scale_fn, is_write):
                nonlocal psA_free, psB_free, acc_last
                h_last = None
                gu_done = []
                for m in range(4):
                    sl = m % 2
                    e_g = B.op("pool", lambda e, m=m, sl=sl, g=g_dram:
                               e.dma_start(
                                   out=wg_v[sl].rearrange("p (k i) -> p k i",
                                                          k=NKC),
                                   in_=g[m].rearrange("(k p) i -> p k i",
                                                      p=128)),
                               waits=[wstate["g_free"][sl], router_pe],
                               inc="s_wg0" if sl == 0 else "s_wg1", n=16)
                    e_u = B.op("pool", lambda e, m=m, sl=sl, u=u_dram:
                               e.dma_start(
                                   out=wu_v[sl].rearrange("p (k i) -> p k i",
                                                          k=NKC),
                                   in_=u[m].rearrange("(k p) i -> p k i",
                                                      p=128)),
                               waits=[wstate["u_free"][sl]],
                               inc="s_wu0" if sl == 0 else "s_wu1", n=16)
                    e_gmm = e_umm = None
                    for th in range(2):
                        toff = th * 1024
                        def gmm(e, sl=sl, toff=toff):
                            r = None
                            for nt in range(2):
                                for kk in range(NKC):
                                    r = e.matmul(
                                        pA[:, nt * 512:(nt + 1) * 512],
                                        wg_v[sl][:, ts(kk)],
                                        xv[:, kk, toff + nt * 512:
                                           toff + (nt + 1) * 512],
                                        start=(kk == 0), stop=(kk == NKC - 1))
                            return r
                        e_gmm = B.op("pe", gmm,
                                     waits=[e_g, xh[th], psA_free],
                                     inc="s_pe")
                        def umm(e, sl=sl, toff=toff):
                            r = None
                            for nt in range(2):
                                for kk in range(NKC):
                                    r = e.matmul(
                                        pB[:, nt * 512:(nt + 1) * 512],
                                        wu_v[sl][:, ts(kk)],
                                        xv[:, kk, toff + nt * 512:
                                           toff + (nt + 1) * 512],
                                        start=(kk == 0), stop=(kk == NKC - 1))
                            return r
                        e_umm = B.op("pe", umm, waits=[e_u, psB_free],
                                     inc="s_pe")
                        ssl = th
                        e_sil = B.op("act", lambda e, ssl=ssl: e.activation(
                            out=tmps_v[ssl], in_=pA[:], func=AF.Silu),
                            waits=[e_gmm, wstate["tmps_free"][ssl]],
                            inc="s_act")
                        psA_free = e_sil
                        e_h = B.op("dve", lambda e, m=m, toff=toff, ssl=ssl:
                                   e.tensor_tensor(
                                       out=h_v[:, m * T + toff:
                                               m * T + toff + 1024],
                                       in0=tmps_v[ssl], in1=pB[:],
                                       op=ALU.mult),
                                   waits=[e_sil, e_umm, wstate["h_free"]],
                                   inc="s_dve")
                        psB_free = e_h
                        wstate["tmps_free"][ssl] = e_h
                        h_last = e_h
                    wstate["g_free"][sl] = e_gmm
                    wstate["u_free"][sl] = e_umm
                e_dw = B.op("pool", lambda e, dd=d_dram: e.dma_start(
                    out=wd_v.rearrange("p (k h) -> p k h", k=4),
                    in_=dd.rearrange("(k p) h -> p k h", p=128)),
                    waits=[wstate["d_free"]], inc="s_wdn", n=16)
                d_pe_last = None
                for tc in range(NTC):
                    for hf in range(2):
                        s2 = (tc * 2 + hf) % 2
                        hoff = hf * 1024
                        def dmm(e, tc=tc, s2=s2, hoff=hoff):
                            r = None
                            for nt in range(2):
                                for kk in range(4):
                                    r = e.matmul(
                                        pC[s2][:, nt * 512:(nt + 1) * 512],
                                        h_v[:, kk * T + tc * 128:
                                            kk * T + (tc + 1) * 128],
                                        wd_v[:, kk * D + hoff + nt * 512:
                                             kk * D + hoff + (nt + 1) * 512],
                                        start=(kk == 0), stop=(kk == 3))
                            return r
                        e_dmm = B.op("pe", dmm,
                                     waits=[e_dw, h_last, psC_free[s2]],
                                     inc="s_pe")
                        d_pe_last = e_dmm
                        e_sc = B.op("act", lambda e, s2=s2, sc=scale_fn(tc):
                                    e.activation(out=tmpd_v[s2], in_=pC[s2][:],
                                                 func=AF.Copy, scale=sc),
                                    waits=[e_dmm, wstate["tmpd_free"][s2],
                                           c_ready, sg_ready],
                                    inc="s_act")
                        psC_free[s2] = e_sc
                        dst = accb[:, tc * D + hoff:tc * D + hoff + 1024]
                        if is_write:
                            e_acc = B.op("dve", lambda e, dst=dst, s2=s2:
                                         e.tensor_copy(dst, tmpd_v[s2]),
                                         waits=[e_sc], inc="s_dve")
                        else:
                            e_acc = B.op("dve", lambda e, dst=dst, s2=s2:
                                         e.tensor_add(dst, dst, tmpd_v[s2]),
                                         waits=[e_sc], inc="s_dve")
                        wstate["tmpd_free"][s2] = e_acc
                        acc_last = e_acc
                wstate["h_free"] = d_pe_last
                wstate["d_free"] = d_pe_last

            one_expert([sg_r[m] for m in range(4)], [su_r[m] for m in range(4)],
                       sd_in, lambda tc: sg_tm[:, tc:tc + 1], True)
            for ei in range(n_experts):
                one_expert([eg_r[ei, m] for m in range(4)],
                           [eu_r[ei, m] for m in range(4)], ed_in[ei],
                           lambda tc, ei=ei: c_v[:, tc * 64 + ei:
                                                 tc * 64 + ei + 1], False)
            return acc_last

    def emit_attention(xln_ready):
        with ExitStack() as ph:
            QP = [ph.enter_context(nc.psum_tensor(_u(f"QP{i}"), [128, 2048], F32))
                  for i in range(2)]
            xv = xlnfm[:].rearrange("p (k t) -> p k t", k=NKC)
            qp_free = [None, None]
            w_free = [None, None]
            kv_st_free = [None, None]
            q_ready = None
            kv_stored = []
            plan = ([("k", i, wk_r) for i in range(4)]
                    + [("v", i, wv_r) for i in range(4)]
                    + [("q", i, wq_r) for i in range(16)])
            for j, (kind, idx, wsrc) in enumerate(plan):
                sl = j % 2
                e_w = B.op("pool", lambda e, sl=sl, idx=idx, w=wsrc:
                           e.dma_start(
                               out=wqkv_v[sl].rearrange("p (k i) -> p k i",
                                                        k=NKC),
                               in_=w[idx].rearrange("(k p) i -> p k i", p=128)),
                           waits=[w_free[sl]],
                           inc="s_wg0" if sl == 0 else "s_wg1", n=16)
                def pmm(e, sl=sl):
                    r = None
                    for nt in range(4):
                        for kk in range(NKC):
                            r = e.matmul(
                                QP[sl][:, nt * 512:(nt + 1) * 512],
                                wqkv_v[sl][:, ts(kk)],
                                xv[:, kk, nt * 512:(nt + 1) * 512],
                                start=(kk == 0), stop=(kk == NKC - 1))
                    return r
                e_mm = B.op("pe", pmm, waits=[e_w, xln_ready, qp_free[sl]],
                            inc="s_pe")
                w_free[sl] = e_mm
                if kind == "q":
                    e_ev = B.op("dve", lambda e, sl=sl, idx=idx: e.tensor_copy(
                        q_sb[:, idx * T:(idx + 1) * T], QP[sl][:]),
                        waits=[e_mm], inc="s_dve")
                else:
                    row0 = (0 if kind == "k" else 512) + idx * 128
                    e_ev = B.op("dve", lambda e, sl=sl: e.tensor_copy(
                        kvst_v[sl], QP[sl][:]),
                        waits=[e_mm, kv_st_free[sl]], inc="s_dve")
                    e_st = B.op("sp", lambda e, sl=sl, row0=row0: e.dma_start(
                        out=kv_scr[row0:row0 + 128, :], in_=kvst_v[sl]),
                        waits=[e_ev], inc="s_st", n=16)
                    kv_st_free[sl] = e_st
                    kv_stored.append(e_st)
                qp_free[sl] = e_ev
                if kind == "q":
                    q_ready = e_ev
        B.phase_barrier()

        e_k = B.op("sp", lambda e: e.dma_start(
            out=k_v.rearrange("p (c t) -> p c t", c=4),
            in_=kv_scr[0:512, :].rearrange("(c p) t -> p c t", p=128)),
            waits=kv_stored[-1:], inc="s_xa", n=16)
        e_msb = B.op("pool", lambda e: e.dma_start(
            out=msb_v.rearrange("p (k j) -> p k j", k=4),
            in_=m_aug_in.rearrange("(k p) j -> p k j", p=128)),
            inc="s_gwt", n=16)
        vT_ready = None
        with ExitStack() as ph:
            TPv = ph.enter_context(nc.psum_tensor(_u("TPv"), [128, 2048], BF16))
            vtmp_free = None
            tpv_free = None
            for vc in range(4):
                e_vl = B.op("sp", lambda e, vc=vc: e.dma_start(
                    out=vtmp_v, in_=kv_scr[512 + vc * 128:512 + (vc + 1) * 128, :]),
                    waits=[kv_stored[-1], vtmp_free], inc="s_xa", n=16)
                def vtr(e):
                    r = None
                    for tc in range(NTC):
                        r = e.transpose(TPv[:, ts(tc)], vtmp_v[:, ts(tc)],
                                        idn_bf[:])
                    return r
                e_vt = B.op("pe", vtr, waits=[e_vl, tpv_free], inc="s_pe")
                vtmp_free = e_vt
                def vev(e, vc=vc):
                    r = None
                    for tc in range(NTC):
                        r = e.tensor_copy(
                            R1b[:, vT_off + tc * 520 + vc * 130:
                                vT_off + tc * 520 + vc * 130 + 130]
                            .rearrange("p (h d) -> p h d", h=2),
                            TPv[:, tc * 128:tc * 128 + 128]
                            .rearrange("p (h d) -> p h d", h=2)[:, :, 0:64]
                            if False else
                            TPv[:, tc * 128:tc * 128 + 128]
                            .rearrange("p (h d) -> p h d", h=2))
                    return r
                def vev2(e, vc=vc):
                    r = None
                    for tc in range(NTC):
                        dst = R1b[:, vT_off + tc * 520 + vc * 130:
                                  vT_off + tc * 520 + (vc + 1) * 130]
                        r = e.tensor_copy(
                            dst.rearrange("p (h d) -> p h d", d=65)[:, :, 0:64],
                            TPv[:, tc * 128:(tc + 1) * 128]
                            .rearrange("p (h d) -> p h d", h=2))
                    return r
                e_vv = B.op("dve", vev2, waits=[e_vt], inc="s_dve")
                tpv_free = e_vv
                vT_ready = e_vv
            def vones(e):
                r = None
                for tc in range(NTC):
                    dst = R1b[:, vT_off + tc * 520:vT_off + (tc + 1) * 520]
                    r = e.memset(
                        dst.rearrange("p (h d) -> p h d", d=65)[:, :, 64:65],
                        1.0)
                return r
            vT_ready = B.op("dve", vones, waits=[vT_ready], inc="s_dve")

            MP = [ph.enter_context(nc.psum_tensor(_u(f"MP{i}"), [128, 1024], F32))
                  for i in range(2)]
            mp_free = [None, None]
            att_mem = []
            prev_rdw = None
            dbg_hold = [None]
            for g in range(4):
                sig_ready = None
                sig_waits = [att_mem[-1] if att_mem else None, dbg_hold[0]]
                prev_e2 = None
                for i in range(4):
                    qc = 4 * g + i
                    for hv in range(2):
                        off = hv * 1024
                        qs = q_sb[:, qc * T + off:qc * T + off + 1024]
                        B.op("dve", lambda e, qs=qs: e.tensor_scalar_min(
                            t3_v[0], qs, 0.0),
                            waits=[q_ready, prev_e2, sig_ready] + sig_waits)
                        e_ex = B.op("dve", lambda e, qs=qs: e.tensor_scalar_max(
                            t3_v[1], qs, 0.0), waits=[sig_ready], inc="s_dve")
                        e_e2 = B.op("act", lambda e: e.activation(
                            out=t3_v[2], in_=t3_v[0], func=AF.Exp),
                            waits=[e_ex], inc="s_act")
                        prev_e2 = e_e2
                        sig_ready = B.op("dve", lambda e, i=i, off=off:
                                         e.scalar_tensor_tensor(
                                             out=sig_v[:, i * T + off:
                                                       i * T + off + 1024],
                                             in0=t3_v[2], scalar=-1.0,
                                             in1=t3_v[1],
                                             op0=ALU.add, op1=ALU.add),
                                         waits=[e_e2], inc="s_dve")
                mv = msb_v.rearrange("p (k j) -> p k j", k=4)
                for tc in range(NTC):
                    s2 = tc % 2
                    def mmm(e, g=g, tc=tc, s2=s2):
                        r = None
                        for kk in range(4):
                            r = e.matmul(MP[s2][:, 0:512],
                                         sig_v[:, kk * T + tc * 128:
                                               kk * T + (tc + 1) * 128],
                                         mv[:, kk, 0:512],
                                         start=(kk == 0), stop=(kk == 3))
                        for kk in range(4):
                            r = e.matmul(MP[s2][:, 512:513],
                                         sig_v[:, kk * T + tc * 128:
                                               kk * T + (tc + 1) * 128],
                                         mv[:, kk, 512:513],
                                         start=(kk == 0), stop=(kk == 3))
                        return r
                    e_mm = B.op("pe", mmm,
                                waits=[sig_ready, e_msb, mp_free[s2]],
                                inc="s_pe")
                    if debug and g == 0 and tc == 0:
                        e_n0 = B.op("dve", lambda e, s2=s2: e.tensor_copy(
                            t3_v[0].rearrange("p f -> p f") if False else
                            R4f[:, 1024:2048][:, 0:513], MP[s2][:, 0:513]),
                            waits=[e_mm], inc="s_dve")
                        e_n0d = B.op("sp", lambda e: e.dma_start(
                            out=dbg["num0"], in_=R4f[:, 1024:2048]),
                            waits=[e_n0], inc="s_dbg", n=16)
                        dbg_hold[0] = e_n0d
                    e_rd = B.op("dve", lambda e, s2=s2: e.reciprocal(
                        rstw[:, 4:5], MP[s2][:, 512:513]),
                        waits=[e_mm, prev_rdw], inc="s_dve")
                    e_rdw = B.op("dve", lambda e: e.tensor_mul(
                        rstw[:, 5:6], rstw[:, 4:5], wb_c),
                        waits=[e_rd, e_wb,
                               att_mem[-1] if att_mem else None],
                        inc="s_dve")
                    prev_rdw = e_rdw
                    e_am = B.op("act", lambda e, g=g, tc=tc, s2=s2:
                                e.activation(
                                    out=accb[:, tc * D + g * 512:
                                             tc * D + (g + 1) * 512],
                                    in_=MP[s2][:, 0:512], func=AF.Copy,
                                    scale=rstw[:, 5:6]),
                                waits=[e_rdw, vT_ready], inc="s_act")
                    mp_free[s2] = e_am
                    att_mem.append(e_am)
            att_mem_last = att_mem[-1]
            if debug:
                att_mem_last = B.op("sp", lambda e: e.dma_start(
                    out=dbg["amem"], in_=accb[:]),
                    waits=[att_mem_last], inc="s_dbg", n=16)


        with ExitStack() as ph:
            QN2 = [ph.enter_context(nc.psum_tensor(_u(f"QN{i}"), [1, T], F32))
                   for i in range(2)]
            qsq2 = [E_v[0:64, :], E_v2[0:64, :]]
            qb_ones = B.sbuf(_u("qb_ones"), [64, 1], BF16)
            e_qb1 = B.op("dve", lambda e: e.memset(qb_ones[:], 1.0),
                         inc="s_dve")
            items = []
            for h in range(8):
                kb2 = (h % 2) * 64
                items.append((k_v.rearrange("p (c t) -> p c t", c=4)
                              [kb2:kb2 + 64, h // 2, :], e_k, 32 + h))
            for gh in range(32):
                qc2, qb2 = gh // 2, (gh % 2) * 64
                items.append((q_sb[qb2:qb2 + 64, qc2 * T:(qc2 + 1) * T],
                              q_ready, 0 + gh))
            e_nm_l = [None] * 40
            e_rx_l = [None] * 40
            for i, (src, dep, dsti) in enumerate(items):
                bb = i % 2
                e_q2 = B.op("act", lambda e, src=src, bb=bb: e.activation(
                    out=qsq2[bb], in_=src, func=AF.Square),
                    waits=[dep, e_nm_l[i - 2] if i >= 2 else None],
                    inc="s_act")
                def nmm(e, bb=bb):
                    r = None
                    for nt in range(4):
                        r = e.matmul(QN2[bb][:, nt * 512:(nt + 1) * 512],
                                     qb_ones[:],
                                     qsq2[bb][:, nt * 512:(nt + 1) * 512],
                                     start=True, stop=True)
                    return r
                e_nm_l[i] = B.op("pe", nmm,
                                 waits=[e_q2, e_qb1,
                                        e_rx_l[i - 2] if i >= 2 else None],
                                 inc="s_pe")
                e_rx_l[i] = B.op("dve", lambda e, bb=bb, dsti=dsti:
                                 e.reduce_max(srow[0:1, dsti:dsti + 1],
                                              QN2[bb][:], axis=AX),
                                 waits=[e_nm_l[i]], inc="s_dve")
            qn_free = e_rx_l[-1]
        with ExitStack() as ph:
            PBC = ph.enter_context(nc.psum_tensor(_u("PBC"), [128, 32], F32))
            e_rep = B.op("dve", lambda e: e.tensor_copy(
                srow[0:1, 40:72].rearrange("p (g h) -> p g h", g=4),
                srow[0:1, 32:40].unsqueeze(1).to_broadcast([1, 4, 8])), waits=[qn_free], inc="s_dve")
            e_prod = B.op("dve", lambda e: e.tensor_mul(
                srow[0:1, 40:72], srow[0:1, 40:72], srow[0:1, 0:32]),
                waits=[e_rep], inc="s_dve")
            e_ln2 = B.op("act", lambda e: e.activation(
                out=srow[0:1, 0:32], in_=srow[0:1, 40:72], func=AF.Ln),
                waits=[e_prod], inc="s_act")
            e_bnd = B.op("act", lambda e: e.activation(
                out=srow[0:1, 40:72], in_=srow[0:1, 0:32], func=AF.Exp,
                scale=0.5, bias=LN8), waits=[e_ln2], inc="s_act")
            e_neg = B.op("dve", lambda e: e.tensor_scalar_mul(
                srow[0:1, 0:32], srow[0:1, 40:72], -1.0),
                waits=[e_bnd], inc="s_dve")
            e_bc = B.op("pe", lambda e: e.matmul(
                PBC[:], ones_row, srow[0:1, 0:32], start=True, stop=True),
                waits=[e_neg], inc="s_pe")
            e_nb = B.op("dve", lambda e: e.tensor_copy(negm_b[:], PBC[:]),
                        waits=[e_bc], inc="s_dve")

        att_last = att_mem_last
        prev_stt = None
        prev_ad = None
        ghs = list(gh_list if gh_list is not None else range(gh_limit))
        for ghi, gh in enumerate(ghs):
            g, h = gh // 8, gh % 8
            qc, qb = gh // 2, (gh % 2) * 64
            kc, kb = h // 2, (h % 2) * 64
            with ExitStack() as ph:
                ST = [ph.enter_context(nc.psum_tensor(_u(f"ST{i}"),
                                                      [128, 1024], F32))
                      for i in range(2)]
                ADp = ph.enter_context(nc.psum_tensor(_u("AD"), [65, T], F32))
                st_free = [None, None]
                ad_last = prev_ad
                Ev = [E_v, E_v2]
                e_exp = [[None, None] for _ in range(NTC)]
                e_av = [[None, None] for _ in range(NTC)]

                def emit_avm(tc, half):
                    def avm(e, tc=tc, half=half, h=h, ADp=ADp):
                        rr = None
                        for nt in (2 * half, 2 * half + 1):
                            rr = e.matmul(
                                ADp[:, nt * 512:(nt + 1) * 512],
                                R1b[:, vT_off + tc * 520 + h * 65:
                                    vT_off + tc * 520 + h * 65 + 65],
                                Ev[tc % 2][:, nt * 512:(nt + 1) * 512],
                                start=(tc == 0), stop=(tc == NTC - 1))
                        return rr
                    e_av[tc][half] = B.op(
                        "pe", avm,
                        waits=[e_exp[tc][half], vT_ready,
                               prev_stt if tc == 0 else None],
                        inc="s_pe")

                for tc in range(NTC):
                    for half in range(2):
                        sl = half
                        def smm(e, tc=tc, half=half, sl=sl, ST=ST,
                                kb=kb, kc=kc, qb=qb, qc=qc):
                            rr = None
                            for nt in range(2):
                                so = half * 1024 + nt * 512
                                rr = e.matmul(
                                    ST[sl][:, nt * 512:(nt + 1) * 512],
                                    k_v[kb:kb + 64, kc * T + tc * 128:
                                        kc * T + (tc + 1) * 128],
                                    q_sb[qb:qb + 64,
                                         qc * T + so:qc * T + so + 512],
                                    start=True, stop=True)
                            return rr
                        e_st = B.op("pe", smm,
                                    waits=[q_ready, e_k, st_free[sl], e_nb],
                                    inc="s_pe")
                        e_exp[tc][half] = B.op(
                            "act", lambda e, half=half, sl=sl, gh=gh,
                            tc=tc, ST=ST: e.activation(
                                out=Ev[tc % 2][:, half * 1024:
                                               (half + 1) * 1024],
                                in_=ST[sl][:], func=AF.Exp,
                                scale=0.125,
                                bias=negm_b[:, gh:gh + 1]),
                            waits=[e_st, ad_last,
                                   e_av[tc - 2][half] if tc >= 2 else None],
                            inc="s_act")
                        st_free[sl] = e_exp[tc][half]
                    if tc >= 1:
                        emit_avm(tc - 1, 0)
                        emit_avm(tc - 1, 1)
                emit_avm(NTC - 1, 0)
                emit_avm(NTC - 1, 1)
                ad_last = e_av[NTC - 1][1]
                if debug and ghi == 0:
                    B.op("sp", lambda e: e.dma_start(
                        out=dbg["Edump"], in_=E_v), waits=[ad_last],
                        inc="s_dbg", n=16)
                e_ad = B.op("dve", lambda e, ADp=ADp: e.tensor_copy(
                    adsb, ADp[:]), waits=[ad_last], inc="s_dve")
                if debug and ghi == 0:
                    e_ad = B.op("sp", lambda e: e.dma_start(
                        out=dbg["adsb0"], in_=adsb), waits=[e_ad],
                        inc="s_dbg", n=16)
                prev_ad = e_ad
            with ExitStack() as ph2:
                dummy = ph2.enter_context(nc.psum_tensor(_u("dm"),
                                                         [128, 2048], F32))
                TP = ph2.enter_context(nc.psum_tensor(_u("TP"),
                                                      [128, 2048], F32))
                def adtr(e, TP=TP):
                    r = None
                    for tc in range(NTC):
                        r = e.transpose(TP[:, tc * 128:tc * 128 + 65],
                                        adsb[:, ts(tc)], idn_f[0:65, 0:65])
                    return r
                e_tr = B.op("pe", adtr, waits=[e_ad, c_idnf], inc="s_pe")
                e_rz1 = B.op("dve", lambda e, TP=TP: e.reciprocal(
                    rzs16[:, 0:16],
                    TP[:].rearrange("p (c x) -> p c x", x=128)[:, :, 64:65]
                    .squeeze(2)), waits=[e_tr], inc="s_dve")
                e_rzs = B.op("dve", lambda e: e.tensor_scalar(
                    out=rzs16[:, 16:32], in0=rzs16[:, 0:16],
                    scalar1=omwb_c, scalar2=None, op0=ALU.mult),
                    waits=[e_rz1, e_omwb], inc="s_dve")
                for tc in range(NTC):
                    dsl = accb[:, tc * D + g * 512 + h * 64:
                               tc * D + g * 512 + h * 64 + 64]
                    prev_stt = B.op("dve", lambda e, tc=tc, dsl=dsl, TP=TP:
                                    e.scalar_tensor_tensor(
                                        out=dsl,
                                        in0=TP[:, tc * 128:tc * 128 + 64],
                                        scalar=rzs16[:, 16 + tc:17 + tc],
                                        in1=dsl,
                                        op0=ALU.mult, op1=ALU.add),
                                    waits=[e_rzs, att_mem_last], inc="s_dve")
                att_last = prev_stt
                if debug and ghi == 0:
                    prev_stt = B.op("sp", lambda e: e.dma_start(
                        out=dbg["attmid"], in_=accb[:]), waits=[prev_stt],
                        inc="s_dbg", n=16)
        return att_last

    with ExitStack() as phb:
        tps = [phb.enter_context(nc.psum_tensor(f"tps{i}", [128, 2048], BF16))
               for i in range(2)]
        xln_ready = emit_boundary(x_in, None, None, [], None, tps=tps)
    if debug:
        B.op("sp", lambda e: e.dma_start(out=dbg["xlnfm"], in_=xlnfm[:]),
             waits=[xln_ready], inc="s_dbg", n=16)

    acc_ready = None
    if phases >= 2:
        c_ready, sg_ready, router_pe_last = emit_router(
            xln_ready, xln_half=bstate["xln_half"])
        acc_ready = emit_ffn(xln_ready, c_ready, sg_ready, router_pe_last,
                             xln_half=bstate["xln_half"])
        if debug:
            B.op("sp", lambda e: e.dma_start(out=dbg["c"], in_=c_v),
                 waits=[c_ready], inc="s_dbg", n=16)
            B.op("sp", lambda e: e.dma_start(out=dbg["lg"], in_=lg_v),
                 waits=[c_ready], inc="s_dbg", n=16)
            B.op("sp", lambda e: e.dma_start(out=dbg["sg"], in_=sg_tm[:]),
                 waits=[sg_ready], inc="s_dbg", n=16)
            B.op("sp", lambda e: e.dma_start(out=dbg["acc"], in_=accb[:]),
                 waits=[acc_ready], inc="s_dbg", n=16)

    if phases >= 3:
        B.phase_barrier()
        with ExitStack() as phb:
            tps = [phb.enter_context(
                nc.psum_tensor(f"tp3{i}", [128, 2048], BF16)) for i in range(2)]
            xln_ready = emit_boundary(x_in, accb[:], 0.5, [acc_ready], x_res,
                                      tps=tps)
        if debug:
            B.op("sp", lambda e: e.dma_start(out=dbg["xln2"], in_=xlnfm[:]),
                 waits=[xln_ready], inc="s_dbg", n=16)

    att_ready = None
    if phases >= 4:
        B.phase_barrier()
        att_ready = emit_attention(xln_ready)
        if debug:
            B.op("sp", lambda e: e.dma_start(out=dbg["q"], in_=q_sb[:]),
                 waits=[att_ready], inc="s_dbg", n=16)
            B.op("sp", lambda e: e.dma_start(out=dbg["adsb"], in_=adsb),
                 waits=[att_ready], inc="s_dbg", n=16)
            B.op("sp", lambda e: e.dma_start(out=dbg["negm"], in_=negm_b[:]),
                 waits=[att_ready], inc="s_dbg", n=16)
            B.op("sp", lambda e: e.dma_start(out=dbg["sig"], in_=sig_v),
                 waits=[att_ready], inc="s_dbg", n=16)
            B.op("sp", lambda e: e.dma_start(out=dbg["att"], in_=accb[:]),
                 waits=[att_ready], inc="s_dbg", n=16)

    if phases >= 5:
        B.phase_barrier()
        with ExitStack() as phb:
            tps = [phb.enter_context(
                nc.psum_tensor(f"tp5{i}", [128, 2048], BF16)) for i in range(2)]
            xln_ready = emit_boundary(x_res, accb[:], 1.0, [att_ready],
                                      x_res2, tps=tps)

    if phases >= 6:
        c_ready, sg_ready, router_pe_last = emit_router(
            xln_ready, xln_half=bstate["xln_half"], pre=att_ready)
        acc_ready = emit_ffn(xln_ready, c_ready, sg_ready, router_pe_last,
                             xln_half=bstate["xln_half"])

    if phases >= 7:
        out_last = emit_boundary(x_res2, accb[:], 0.5, [acc_ready], out_d,
                                 final=True, pre_ld=c_ready)
        B.wait("sp", [out_last])

    B.wait("sp", [(s, B.cnt[s]) for s in
                  ("s_st", "s_xa", "s_cst", "s_dbg")])
    B.wait("pool", [(s, B.cnt[s]) for s in
                    ("s_gwt", "s_wg0", "s_wg1", "s_wu0", "s_wu1", "s_wdn")])

    B.finalize()
    B.es.close()
    return nc, dbg



def _prep_inputs(ins):
    f32 = np.float32
    d = {}
    d["wq_r"] = np.ascontiguousarray(
        ins["L0_attn_Wq"].T.reshape(D, 16, 128).transpose(1, 0, 2))
    d["wk_r"] = np.ascontiguousarray(
        ins["L0_attn_Wk"].T.reshape(D, 4, 128).transpose(1, 0, 2))
    d["wv_r"] = np.ascontiguousarray(
        ins["L0_attn_Wv"].T.reshape(D, 4, 128).transpose(1, 0, 2))
    d["m_aug"] = np.ascontiguousarray(np.concatenate(
        [ins["L0_attn_memory"], ins["L0_attn_memory_norm"]], axis=1))
    d["mw"] = np.asarray(ins["L0_attn_memory_weight"]).reshape(1, 1).astype(f32)
    d["gate_w"] = ins["L0_ffn_gate_w"]
    d["eg_r"] = np.ascontiguousarray(
        ins["L0_ffn_expert_gate_w"].reshape(NE, D, 4, 128).transpose(0, 2, 1, 3))
    d["eu_r"] = np.ascontiguousarray(
        ins["L0_ffn_expert_up_w"].reshape(NE, D, 4, 128).transpose(0, 2, 1, 3))
    d["ed"] = ins["L0_ffn_expert_down_w"]
    d["sg_r"] = np.ascontiguousarray(
        ins["L0_ffn_shared_gate_w"].reshape(D, 4, 128).transpose(1, 0, 2))
    d["su_r"] = np.ascontiguousarray(
        ins["L0_ffn_shared_up_w"].reshape(D, 4, 128).transpose(1, 0, 2))
    d["sd"] = ins["L0_ffn_shared_down_w"]
    d["seg"] = np.ascontiguousarray(
        ins["L0_ffn_shared_expert_gate_w"].reshape(16, 128).T)
    consts = np.zeros((128, 8), f32)
    consts[:, 0] = 1.0
    consts[:, 1] = -1.0
    consts[:, 2] = 0.5
    consts[:, 3] = 1.0 / D
    consts[:, 4] = 1e-5
    consts[:, 5] = 0.125
    consts[:, 6] = LN8
    d["consts_i"] = consts
    d["idn_i"] = np.eye(128, dtype=f32)
    d["ones_row_i"] = np.ones((1, 128), f32)
    return d


def kernel(**inputs):
    nc, _ = build_program()
    shared = _prep_inputs(inputs)
    x = inputs["L0_attn_x"]
    in_maps = []
    for c in range(8):
        m = dict(shared)
        m["x"] = np.ascontiguousarray(x[c])
        in_maps.append(m)
    res = run_bass_kernel_spmd(nc, in_maps, core_ids=list(range(8)))
    return np.stack([res.results[c]["out"] for c in range(8)], axis=0)
